# version 14
# baseline (speedup 1.0000x reference)
import numpy as np
import ml_dtypes
import concourse.bass as bass
import concourse.mybir as mybir
from concourse.bass_utils import run_bass_kernel_spmd

F32 = mybir.dt.float32
F16 = mybir.dt.float16
BF16 = mybir.dt.bfloat16
I8 = mybir.dt.int8
AF = mybir.ActivationFunctionType
ALU = mybir.AluOpType
AX = mybir.AxisListType

SEQ = 2048
DM = 1024
DH = 4096
NDC = 8
NCC = 32
FT = 256
FW = FT + 2
NFT = SEQ // FT
QT = 512
NQT = SEQ // QT
NKC = SEQ // 128
CSHIFT = -24.0
PB = 512

ENGS = ("sync", "tensor", "vector", "scalar", "gpsimd")
PHASES = "full"
DEBUG = False
DBGS = False
DBGS_Q1 = True


class Builder:
    def __init__(self):
        self.nc = bass.Bass(target_bir_lowering=False,
                            detect_race_conditions=False)
        self.ops = {e: [] for e in ENGS}
        self.cnt = {}
        self.bank_ready = {}
        self.bank_free = {}
        self._build()

    def v(self, s):
        return self.cnt.get(s, 0)

    def E(self, eng, build, waits=(), inc=None):
        waits = [w for w in waits if w is not None]
        if inc is not None:
            self.cnt[inc[0]] = self.cnt.get(inc[0], 0) + inc[1]
        self.ops[eng].append((tuple(waits), build, inc))
        return None if inc is None else (inc[0], self.cnt[inc[0]])

    def dma(self, eng, sem, dst, src, waits=(), slow=False):
        def b(e):
            if slow:
                return e.dma_start(dst, src, allow_slow_non_contiguous=True)
            return e.dma_start(dst, src)
        self.E(eng, b, waits, (sem, 16))
        return (sem, self.v(sem))

    def wbank(self, i, waits):
        w = list(waits)
        if i in self.bank_free:
            w.append(self.bank_free[i])
        return w

    def rbank(self, i, waits):
        w = list(waits)
        if i in self.bank_ready:
            w.append(self.bank_ready[i])
        return w

    def vflush(self, waits=(), inc=None):
        S = self.S
        return self.E("vector", lambda V: V.tensor_copy(
            S["junkf"][:, 0:128], S["ident"][:, :]), waits, inc)

    def barrier_waits(self):
        return [(s, self.v(s)) for s in
                ("tT", "tV", "tS", "tG", "dS", "dC", "dG") if self.v(s) > 0]

    def phase_barrier(self):
        bw = self.barrier_waits()
        for e in ENGS:
            self.E(e, lambda eng: eng.nop(), bw)

    def _build(self):
        nc = self.nc
        E, dma, v = self.E, self.dma, self.v

        self.xfm = nc.dram_tensor("xfm", [2, DM, SEQ], F32, kind="ExternalInput")
        self.wqkv_r = nc.dram_tensor("wqkv_r", [24, DM, 128], F32,
                                     kind="ExternalInput")
        self.wout = nc.dram_tensor("wout", [DM, DM], F32, kind="ExternalInput")
        self.w1t = nc.dram_tensor("w1t", [DM, DH], F32, kind="ExternalInput")
        self.w2t = nc.dram_tensor("w2t", [DH, DM], F32, kind="ExternalInput")
        self.bqr = nc.dram_tensor("bqr", [128, 24], F32, kind="ExternalInput")
        self.bor = nc.dram_tensor("bor", [128, 8], F32, kind="ExternalInput")
        self.cwr = nc.dram_tensor("cwr", [128, 96], F32, kind="ExternalInput")
        self.cbr = nc.dram_tensor("cbr", [128, 32], F32, kind="ExternalInput")
        self.ofm = nc.dram_tensor("ofm", [2, DM, SEQ], F32, kind="ExternalOutput")
        self.t1s = nc.dram_tensor("t1s", [8, 128, 4096], BF16, kind="Internal")
        if DBGS:
            self.dbg_q1 = nc.dram_tensor("dbg_q1", [4, 4, 5, 512], F32,
                                         kind="ExternalOutput")
            self.dbg_q2 = nc.dram_tensor("dbg_q2", [4, 8, 3, 256], F32,
                                         kind="ExternalOutput")
        self.t2s = nc.dram_tensor("t2s", [8, 128, 4096], BF16, kind="Internal")
        if DEBUG:
            self.dbg_st = nc.dram_tensor("dbg_st", [128, 2120], F32,
                                         kind="ExternalOutput")

        S = {}

        def sb(name, shape, dtype, off):
            t = nc.alloc_sbuf_tensor_at(name, shape, dtype, offset=off)
            S[name] = t
            return t
        self.S = S

        C = 18432
        ident = sb("ident", [128, 128], F32, C + 0)
        onesm = sb("onesm", [128, 128], F32, C + 512)
        ones16 = sb("ones16", [128, 128], F16, C + 1024)
        onesbf = sb("onesbf", [128, 128], BF16, C + 1280)
        cneg = sb("cneg", [128, 1], F32, C + 1536)
        epsc = sb("epsc", [128, 1], F32, C + 1568)
        bq = sb("bq", [128, 24], F32, C + 1600)
        bo = sb("bo", [128, 8], F32, C + 1696)
        cw = sb("cw", [128, 96], F32, C + 1728)
        cb = sb("cb", [128, 32], F32, C + 2112)
        mw1c = sb("mw1c", [128, 1], F32, C + 2240)
        mw2ch = sb("mw2ch", [128, 1], F32, C + 2272)
        sw1 = sb("sw1", [128, 1], F32, C + 2304)
        sw2 = sb("sw2", [128, 1], F32, C + 2336)
        wacc = sb("wacc", [128, 16], F32, C + 2368)
        xsave = sb("xsave", [128, 8], F32, C + 2432)
        wstat = sb("wstat", [128, 8], F32, C + 2496)
        xr = sb("xr", [128, NDC * SEQ], F32, C + 2560)

        D = C + 2560 + 65536
        Q1W = 512
        NQ1C = SEQ // Q1W
        ring1 = sb("ring1", [128, 2 * 4096], BF16, D)
        ring2 = sb("ring2", [128, 2 * 4096], BF16, D + 16384)
        k1full = sb("k1full", [128, NDC * SEQ], BF16, D + 32768)
        hs = sb("hs", [128, 2 * FW], F32, D + 65536)
        cbuf = sb("cbuf", [128, 2 * FT], F32, D + 67616)
        hp = sb("hp", [128, NCC * FT], F32, D + 69664)
        k2i8 = sb("k2i8", [128, NCC * FT], I8, D + 102432)
        k2bf = sb("k2bf", [128, NCC * FT], BF16, D + 110624)
        self.q1stats = {}
        off = D + 69664
        for sn in ("mean_c", "tmp_c", "tmp2_c", "rstd_c", "m1_c", "a_c",
                   "q_c", "g1_c"):
            self.q1stats[sn] = sb(sn, [128, Q1W], F32, off)
            off += 2048
        xcq = sb("xcq", [128, NDC * Q1W], F32, D + 86048)
        k1i8q = sb("k1i8q", [128, NDC * Q1W], I8, D + 102432)
        xh16q = sb("xh16q", [128, NDC * Q1W], F16, D + 110624)
        xq16q = sb("xq16q", [128, NDC * Q1W], F16, D + 118816)
        self.stats = {}
        self.stats["m2_b"] = sb("m2_b", [128, FT], F32, D + 127104)
        self.stats["a2_b"] = sb("a2_b", [128, FT], F32, D + 128128)
        self.stats["s2_b"] = sb("s2_b", [128, FT], F32, D + 139936)
        self.stats["g2_b"] = sb("g2_b", [128, FT], F32, D + 140960)
        junkf = sb("junkf", [128, 128], F32, D + 141984)
        a_tm = sb("a_tm", [128, 4], F32, D + 137344)
        aT = sb("aT", [4, 128], F32, D + 137376)
        arow = sb("arow", [1, 512], F32, D + 137888)
        self.Q1W, self.NQ1C = Q1W, NQ1C
        assert D + 142496 <= 229344

        xnfull = sb("xnfull", [128, NDC * SEQ], BF16, D)
        qt_ = sb("qt_", [128, 4 * SEQ], BF16, D + 32768)
        kt_ = sb("kt_", [128, 4 * SEQ], BF16, D + 49152)
        vge = sb("vge", [128, NKC * 4 * 65], BF16, D + 65536)
        vgo = sb("vgo", [128, NKC * 4 * 128], BF16, D + 73856)
        osb = sb("osb", [128, NQT * 8 * QT], BF16, D + 90240)
        wqblk = sb("wqblk", [128, 2 * NDC * 128], BF16, D + 123008)
        vstage = sb("vstage", [128, QT], F32, D + 127104)
        AST = 256
        ah16 = sb("ah16", [128, NDC * AST], F16, D + 129152)
        aq16 = sb("aq16", [128, NDC * AST], F16, D + 133248)
        self.astats = {}
        off = D + 137344
        for sn in ("amean", "atmp", "atmp2", "arstd"):
            self.astats[sn] = sb(sn, [128, AST], F32, off)
            off += 1024
        assert off <= 229344, off
        pte = sb("pte", [128, QT], BF16, D + 129152)
        pto = sb("pto", [128, QT], BF16, D + 130176)
        zhl = sb("zhl", [128, 2 * QT], BF16, D + 131200)
        pte2 = sb("pte2", [128, QT], BF16, D + 133248)
        pto2 = sb("pto2", [128, QT], BF16, D + 134272)
        pteo = sb("pteo", [128, 2 * QT], BF16, D + 129152)
        pteo2 = sb("pteo2", [128, 2 * QT], BF16, D + 133248)
        zrow = sb("zrow", [128, QT], F32, D + 137344)
        zbsb = sb("zbsb", [128, QT], F32, D + 139392)
        wo_sb = sb("wo_sb", [128, NDC * DM], BF16, D + 65536)
        self.AST = AST
        wqf = sb("wqf", [128, 8 * 1024], F32, D)
        wqf2 = sb("wqf2", [128, 8 * 1024], F32, D + 69664)
        wqb = sb("wqb", [128, 8 * 1024], BF16, D + 32768)
        wqi = sb("wqi", [128, 8 * 1024], I8, D + 49152)

        pb01 = nc.alloc_psum_tensor("pb01", [128, 2 * PB], F32)

        class _BankView:
            def __init__(self, t, off):
                self.t, self.off = t, off

            def __getitem__(self, idx):
                rows, cols = idx
                return self.t[rows, cols.start + self.off:cols.stop + self.off]
        pb = ([_BankView(pb01, 0), _BankView(pb01, PB)] +
              [nc.alloc_psum_tensor("pb%d" % i, [128, PB], F32)
               for i in range(2, 8)])
        self.pb01 = pb01
        self.pb = pb

        mi = E("gpsimd", lambda g: g.memset(ident[:, :], 0.0), inc=("tG", 1))
        E("gpsimd", lambda g: g.affine_select(
            out=ident[:, :], in_=ident[:, :], compare_op=ALU.not_equal,
            fill=1.0, base=0, pattern=[[-1, 128]], channel_multiplier=1),
            waits=[mi])
        E("gpsimd", lambda g: g.memset(onesm[:, :], 1.0))
        E("gpsimd", lambda g: g.memset(ones16[:, :], 1.0))
        E("gpsimd", lambda g: g.memset(onesbf[:, :], 1.0))
        E("gpsimd", lambda g: g.memset(cneg[:, :], CSHIFT))
        E("gpsimd", lambda g: g.memset(epsc[:, :], 1e-5))
        E("gpsimd", lambda g: g.memset(wacc[:, :], 0.0))
        E("gpsimd", lambda g: g.memset(a_tm[:, :], 0.0))
        E("gpsimd", lambda g: g.memset(aT[:, :], 0.0))
        E("gpsimd", lambda g: g.memset(arow[:, :], 0.0))
        E("gpsimd", lambda g: g.memset(wstat[:, :], 0.0))
        E("gpsimd", lambda g: g.memset(junkf[:, :], 0.0), inc=("tG", 1))
        gp_const = ("tG", v("tG"))
        dma("scalar", "dC", bq[:, :], self.bqr[:, :])
        dma("scalar", "dC", bo[:, :], self.bor[:, :])
        dma("scalar", "dC", cw[:, :], self.cwr[:, :])
        w_bias = dma("scalar", "dC", cb[:, :], self.cbr[:, :])

        wqf2 = S["wqf2"]
        bufs = (wqf, wqf2)
        queues = (("sync", "dS"), ("gpsimd", "dG"))

        def load_w(i, waits=()):
            wi, g = divmod(i, 4)
            if wi == 0:
                src = bass.AP(self.w1t, g * 1024,
                              [[DH, 128], [128 * DH, 8], [1, 1024]])
            else:
                src = bass.AP(self.w2t, g * 1024 * DM,
                              [[DM, 128], [128 * DM, 8], [1, 1024]])
            eng, q = queues[i % 2]
            return dma(eng, q, bufs[i % 2][:, :], src, waits=waits)

        red_marks = {}
        for i in range(8):
            vl = load_w(i, waits=[red_marks[i - 2]] if i >= 2 else [])
            red_marks[i] = E("vector", lambda V, i=i: V.tensor_reduce(
                wacc[:, i:i + 1], bufs[i % 2][:, :], AX.X,
                ALU.add, apply_absolute_value=True),
                waits=[vl, gp_const], inc=("tV", 1))
        self.vflush()
        E("vector", lambda V: V.memset(wqb[:, 0:256], 0.0))
        E("vector", lambda V: V.tensor_copy(wqb[:, 0:8], wacc[:, 0:8]))
        self.vflush()
        E("vector", lambda V: V.tensor_tensor(
            wqb[:, 128:136], wacc[:, 0:8], wqb[:, 0:8], ALU.subtract))
        v_hl = self.vflush(inc=("tV", 1))
        E("tensor", lambda T: T.matmul(
            pb[7][:, 0:128], onesbf[:, :], wqb[:, 0:128], start=True, stop=False),
            waits=self.wbank(7, [v_hl]))
        wt2 = E("tensor", lambda T: T.matmul(
            pb[7][:, 0:128], onesbf[:, :], wqb[:, 128:256], start=False, stop=True),
            inc=("tT", 1))
        self.bank_ready[7] = wt2
        E("vector", lambda V: V.tensor_copy(wacc[:, 8:16], pb[7][:, 0:8]),
            waits=self.rbank(7, []))
        vfr = self.vflush(inc=("tV", 1))
        self.bank_free[7] = vfr
        E("vector", lambda V: V.tensor_reduce(
            wstat[:, 0:1], wacc[:, 8:12], AX.X, ALU.add))
        self.vflush()
        E("vector", lambda V: V.tensor_reduce(
            wstat[:, 1:2], wacc[:, 12:16], AX.X, ALU.add))
        self.vflush()
        E("vector", lambda V: V.tensor_scalar(
            wstat[:, 2:4], wstat[:, 0:2], 1.0 / (DM * DH), None, ALU.mult))
        self.vflush()
        E("vector", lambda V: V.tensor_scalar(
            wstat[:, 4:6], wstat[:, 2:4], 1e-5, None, ALU.max))
        self.vflush()
        E("vector", lambda V: V.tensor_scalar(
            mw1c[:, 0:1], wstat[:, 4:5], 1.0 / 127.0, None, ALU.mult))
        E("vector", lambda V: V.tensor_scalar(
            mw2ch[:, 0:1], wstat[:, 5:6], 0.5 / 127.0, None, ALU.mult))
        self.vflush()
        E("vector", lambda V: V.reciprocal(sw1[:, 0:1], wstat[:, 4:5]))
        self.vflush()
        E("vector", lambda V: V.reciprocal(sw2[:, 0:1], wstat[:, 5:6]))
        v_repl = self.vflush(inc=("tV", 1))

        cast_marks = {}
        prev_store = None
        self.scr_marks = {}
        for i in range(8):
            swc = sw1 if i < 4 else sw2
            if i >= 2:
                lw = [cast_marks[i - 2]]
            else:
                lw = [red_marks[6 + (i % 2)]]
            vl = load_w(i, waits=lw)
            buf = bufs[i % 2]
            E("vector", lambda V, swc=swc, buf=buf: V.tensor_scalar(
                buf[:, :], buf[:, :], swc[:, 0:1], 1.4, ALU.mult, ALU.min),
                waits=[vl, v_repl])
            vclip = E("vector", lambda V, buf=buf: V.tensor_scalar(
                buf[:, :], buf[:, :], -1.4, None, ALU.max), inc=("tV", 1))
            cast_marks[i] = E("scalar", lambda Sc, buf=buf: Sc.activation(
                wqi[:, :], buf[:, :], AF.Copy), waits=[vclip], inc=("tS", 1))
            vb = E("vector", lambda V: V.tensor_copy(wqb[:, :], wqi[:, :]),
                   waits=[cast_marks[i]] + ([prev_store] if prev_store else []),
                   inc=("tV", 1))
            wi, g = divmod(i, 4)
            if wi == 0:
                for u in range(2):
                    dst = bass.AP(self.t1s, (2 * g + u) * 128 * 4096,
                                  [[4096, 128], [512, 8], [1, 512]])
                    srcb = bass.AP(wqb, u * 512,
                                   [[8192, 128], [1024, 8], [1, 512]])
                    prev_store = dma("scalar", "dC", dst, srcb, waits=[vb])
                    self.scr_marks[(1, 2 * g + u)] = prev_store
            else:
                dst = bass.AP(self.t2s, (2 * g) * 128 * 4096,
                              [[4096, 128], [128 * 4096, 2], [1, 4096]])
                prev_store = dma("scalar", "dC", dst, wqb[:, :], waits=[vb])
                self.scr_marks[(2, 2 * g)] = prev_store
                self.scr_marks[(2, 2 * g + 1)] = prev_store
        self.dc_scr = ("dC", v("dC"))
        self.ring_prev = {}
        self.ring_marks = {}

        for seq in (0, 1):
            self.phase_barrier()
            self.emit_F(seq, first=True, store_out=(PHASES == "f1"))
            if PHASES == "f1":
                continue
            self.phase_barrier()
            self.emit_A(seq)
            if PHASES == "f1a":
                self.phase_barrier()
                dstr = bass.AP(self.ofm, seq * DM * SEQ,
                               [[SEQ, 128], [128 * SEQ, NDC], [1, SEQ]])
                dma("scalar", "dC", dstr, S["xr"][:, :],
                    waits=self.barrier_waits())
                continue
            self.phase_barrier()
            self.emit_F(seq, first=False, store_out=True)

        if DEBUG:
            bwf = self.barrier_waits()
            dma("scalar", "dC", self.dbg_st[:, :], S["dbgstage"][:, :], waits=bwf)
        fin = [("dC", v("dC")), ("dS", v("dS")), ("dG", v("dG"))]
        for e in ENGS:
            self.E(e, lambda eng: eng.nop(), fin)

    def repl_max(self, m_src, c0, W, dst_b, v_wait):
        E, v, pb = self.E, self.v, self.pb
        S = self.S
        ident, onesm = S["ident"], S["onesm"]
        a_tm, aT, arow = S["a_tm"], S["aT"], S["arow"]
        nblk = (W + 127) // 128
        for b in range(nblk):
            cc0 = c0 + b * 128
            cwd = min(128, c0 + W - cc0)
            bk = 6 + (b % 2)
            tb = E("tensor", lambda T, cc0=cc0, cwd=cwd, bk=bk: T.transpose(
                pb[bk][0:cwd, 0:128],
                m_src[:, cc0:cc0 + cwd], ident[:, :]),
                waits=self.wbank(bk, [v_wait] if b == 0 else []),
                inc=("tT", 1))
            self.bank_ready[bk] = tb
            vr = E("vector", lambda V, cwd=cwd, b=b, bk=bk: V.tensor_reduce(
                a_tm[0:cwd, b:b + 1], pb[bk][0:cwd, 0:128],
                AX.X, ALU.max),
                waits=self.rbank(bk, []), inc=("tV", 1))
            self.bank_free[bk] = vr
            self.vflush()
        v_red = self.vflush(inc=("tV", 1))
        t_at = E("tensor", lambda T, nblk=nblk: T.transpose(
            pb[7][0:nblk, 0:128], a_tm[:, 0:nblk], ident[:, :]),
            waits=self.wbank(7, [v_red]), inc=("tT", 1))
        self.bank_ready[7] = t_at
        E("vector", lambda V, nblk=nblk: V.tensor_copy(
            aT[0:nblk, :], pb[7][0:nblk, 0:128]),
            waits=self.rbank(7, []))
        v_at = self.vflush(inc=("tV", 1))
        self.bank_free[7] = v_at
        full = (W // 128) * 128
        g1 = g2 = None
        if full > 0:
            dsta = bass.AP(arow, c0, [[FW, 1], [128, W // 128], [1, 128]])
            g1 = self.dma("gpsimd", "dG", dsta, aT[0:W // 128, :], waits=[v_at])
        rem = W - full
        if rem > 0:
            dsta = bass.AP(arow, c0 + full, [[FW, 1], [1, rem]])
            g2 = self.dma("gpsimd", "dG", dsta, aT[W // 128:W // 128 + 1, 0:rem],
                          waits=[v_at] if full == 0 else [])
        dg = ("dG", v("dG"))
        t_rep = E("tensor", lambda T, c0=c0, W=W: T.matmul(
            pb[7][:, 0:W], onesm[0:1, :], arow[:, c0:c0 + W],
            start=True, stop=True),
            waits=self.wbank(7, [dg, v_at]), inc=("tT", 1))
        self.bank_ready[7] = t_rep
        v_ab = E("vector", lambda V, c0=c0, W=W, dst_b=dst_b: V.tensor_copy(
            dst_b[:, c0:c0 + W], pb[7][:, 0:W]),
            waits=self.rbank(7, []), inc=("tV", 1))
        self.bank_free[7] = v_ab
        return v_ab

    def emit_F(self, seq, first, store_out):
        E, v, dma, pb, S = self.E, self.v, self.dma, self.pb, self.S
        vx = None
        if first:
            src = bass.AP(self.xfm, seq * DM * SEQ,
                          [[SEQ, 128], [128 * SEQ, NDC], [1, SEQ]])
            bwx = self.barrier_waits() if seq else []
            vx = dma("scalar", "dC", S["xr"][:, :], src, waits=bwx)
        self.q1_marks = {}
        self.q1_xr_guard = None
        self.pending_mm2 = None
        self.cbuf_prev = None
        self.conv_prev = None
        self.evac_prev = None
        self.k2_prev = {}
        self.mm2_mark = None
        self.fidx = getattr(self, 'fidx', -1) + 1
        for ch in range(self.NQ1C):
            self.emit_Q1(ch, vx if ch == 0 else None)
        for t in range(NFT):
            self._F_mm1(seq, t)
            if self.pending_mm2 is not None:
                self.pending_mm2()
                self.pending_mm2 = None
            self._F_quant2(t)
            self._F_mm2_defer(seq, t, store_out)
        self.pending_mm2()
        self.pending_mm2 = None

    def _xr(self, dc, c, w):
        return self.S["xr"][:, dc * SEQ + c: dc * SEQ + c + w]

    def emit_Q1(self, ch, vx):
        E, v, pb, S = self.E, self.v, self.pb, self.S
        q = self.q1stats
        Q1W = self.Q1W
        q0 = ch * Q1W
        xh16q, xq16q, xcq, k1i8q = (S["xh16q"], S["xq16q"], S["xcq"],
                                    S["k1i8q"])
        k1full = S["k1full"]
        ones16, epsc, mw1c = S["ones16"], S["epsc"], S["mw1c"]
        qs = lambda n: q[n][:, 0:Q1W]
        for dc in range(NDC):
            E("scalar", lambda Sc, dc=dc: Sc.activation(
                xh16q[:, dc * Q1W:(dc + 1) * Q1W], self._xr(dc, q0, Q1W),
                AF.Copy),
                waits=[vx] if (dc == 0 and vx is not None) else [])
        for dc in range(NDC):
            E("scalar", lambda Sc, dc=dc: Sc.activation(
                xq16q[:, dc * Q1W:(dc + 1) * Q1W],
                xh16q[:, dc * Q1W:(dc + 1) * Q1W], AF.Square),
                inc=("tS", 1) if dc == NDC - 1 else None)
        s_xh = ("tS", v("tS"))
        self.q1_xr_guard = s_xh
        for dc in range(NDC):
            E("tensor", lambda T, dc=dc: T.matmul(
                pb[6][:, 0:Q1W], ones16[:, :],
                xh16q[:, dc * Q1W:(dc + 1) * Q1W],
                start=(dc == 0), stop=(dc == NDC - 1)),
                waits=self.wbank(6, [s_xh]) if dc == 0 else [])
        for dc in range(NDC):
            E("tensor", lambda T, dc=dc: T.matmul(
                pb[7][:, 0:Q1W], ones16[:, :],
                xq16q[:, dc * Q1W:(dc + 1) * Q1W],
                start=(dc == 0), stop=(dc == NDC - 1)),
                waits=self.wbank(7, []) if dc == 0 else [],
                inc=("tT", 1) if dc == NDC - 1 else None)
        t_sums = ("tT", v("tT"))
        self.bank_ready[6] = t_sums
        self.bank_ready[7] = t_sums
        mw_ = self.rbank(6, [])
        if self.q1_marks:
            mw_.append(self.q1_marks[max(self.q1_marks)])
        E("scalar", lambda Sc: Sc.activation(
            qs("mean_c"), pb[6][:, 0:Q1W], AF.Copy, scale=1.0 / DM),
            waits=mw_)
        s_m = E("scalar", lambda Sc: Sc.activation(
            qs("tmp_c"), pb[7][:, 0:Q1W], AF.Copy, scale=1.0 / DM),
            waits=self.rbank(7, []), inc=("tS", 1))
        self.bank_free[6] = s_m
        self.bank_free[7] = s_m
        E("vector", lambda V: V.tensor_tensor(
            qs("tmp2_c"), qs("mean_c"), qs("mean_c"), ALU.mult), waits=[s_m])
        E("vector", lambda V: V.tensor_tensor(
            qs("tmp_c"), qs("tmp_c"), qs("tmp2_c"), ALU.subtract))
        v_var = E("vector", lambda V: V.tensor_scalar(
            qs("tmp_c"), qs("tmp_c"), 0.0, None, ALU.max), inc=("tV", 1))
        E("scalar", lambda Sc: Sc.activation(
            qs("tmp2_c"), qs("tmp_c"), AF.Ln, bias=epsc[:, 0:1]),
            waits=[v_var])
        s_r = E("scalar", lambda Sc: Sc.activation(
            qs("rstd_c"), qs("tmp2_c"), AF.Exp, scale=-0.5), inc=("tS", 1))
        for dc in range(NDC):
            E("vector", lambda V, dc=dc, q0=q0: V.tensor_tensor(
                xcq[:, dc * Q1W:(dc + 1) * Q1W], self._xr(dc, q0, Q1W),
                qs("mean_c"), ALU.subtract),
                waits=[s_r] if dc == 0 else [])
        v_m1w = E("vector", lambda V: V.tensor_reduce(
            qs("m1_c"),
            bass.AP(xcq, 0, [[NDC * Q1W, 128], [1, Q1W], [Q1W, NDC]]),
            AX.X, ALU.max, apply_absolute_value=True), inc=("tV", 1))
        self.repl_max(q["m1_c"], 0, Q1W, q["a_c"], v_m1w)
        E("vector", lambda V: V.tensor_tensor(
            qs("tmp_c"), qs("a_c"), qs("rstd_c"), ALU.mult))
        E("vector", lambda V: V.tensor_scalar(
            qs("tmp_c"), qs("tmp_c"), 1e-5, None, ALU.max))
        vq_r = E("vector", lambda V: V.reciprocal(
            qs("tmp2_c"), qs("tmp_c")), inc=("tV", 1))
        E("vector", lambda V: V.scalar_tensor_tensor(
            qs("q_c"), qs("tmp2_c"), 127.0, qs("rstd_c"),
            ALU.mult, ALU.mult), waits=[vq_r])
        E("vector", lambda V: V.tensor_scalar(
            qs("g1_c"), qs("tmp_c"), mw1c[:, 0:1], None, ALU.mult))
        for dc in range(NDC):
            E("vector", lambda V, dc=dc: V.tensor_tensor(
                k1i8q[:, dc * Q1W:(dc + 1) * Q1W],
                xcq[:, dc * Q1W:(dc + 1) * Q1W], qs("q_c"), ALU.mult))
        for dc in range(NDC):
            E("vector", lambda V, dc=dc, q0=q0: V.tensor_tensor(
                k1full[:, dc * SEQ + q0:dc * SEQ + q0 + Q1W],
                k1i8q[:, dc * Q1W:(dc + 1) * Q1W], qs("g1_c"), ALU.mult),
                inc=("tV", 1) if dc == NDC - 1 else None)
        self.q1_marks[ch] = ("tV", v("tV"))
        if DBGS and DBGS_Q1:
            dmark = None
            for j, nm in enumerate(("q_c", "g1_c", "a_c", "rstd_c", "mean_c")):
                dmark = self.dma("scalar", "dC",
                                 self.dbg_q1[self.fidx, ch, j, :],
                                 q[nm][0:1, 0:Q1W],
                                 waits=[self.q1_marks[ch]] if j == 0 else [])
            self.q1_dump = dmark
            E("vector", lambda V: V.engine_nop(), waits=[dmark])

    def _F_mm1(self, seq, t):
        E, v, dma, pb, S = self.E, self.v, self.dma, self.pb, self.S
        a0 = t * FT
        lo = max(a0 - 1, 0)
        hi = min(a0 + FT + 1, SEQ)
        W = hi - lo
        c0 = lo - (a0 - 1)
        hs, cbuf, hp = S["hs"], S["cbuf"], S["hp"]
        k1full, ring1 = S["k1full"], S["ring1"]
        cw, cb = S["cw"], S["cb"]
        need_ch = (a0 + FT) // self.Q1W
        v_k1 = self.q1_marks[min(need_ch, self.NQ1C - 1)]
        if t == 0:
            for sl in range(2):
                E("vector", lambda V, sl=sl: V.memset(
                    hs[:, sl * FW:sl * FW + 1], 0.0))
            self.vflush()
        if t == NFT - 1:
            for sl in range(2):
                E("vector", lambda V, sl=sl: V.memset(
                    hs[:, sl * FW + FW - 1:sl * FW + FW], 0.0))
            self.vflush()
        conv_done = {}
        silu_done = {}
        evac_done = {}
        pend_silu = None
        for cc in range(NCC):
            g = cc // 4
            slot = g % 2
            sl = cc % 2
            if cc % 4 == 0:
                self._ring_load(1, g, slot)
            bank = 4 + sl
            waits = []
            if cc == 0:
                waits.append(v_k1)
            if cc % 4 == 0:
                waits.append(self.ring_marks[("r1", slot)])
            if cc >= 2:
                waits.append(evac_done[cc - 2])
            elif self.evac_prev is not None:
                waits.append(self.evac_prev[cc])
            for dc in range(NDC):
                E("tensor", lambda T, dc=dc, cc=cc, slot=slot, bank=bank,
                  lo=lo, W=W, c0=c0: T.matmul(
                      pb[bank][:, c0:c0 + W],
                      ring1[:, slot * 4096 + dc * 512 + (cc % 4) * 128:
                            slot * 4096 + dc * 512 + (cc % 4) * 128 + 128],
                      k1full[:, dc * SEQ + lo:dc * SEQ + lo + W],
                      start=(dc == 0), stop=(dc == NDC - 1)),
                  waits=waits if dc == 0 else [],
                  inc=("tT", 1) if dc == NDC - 1 else None)
            t_mm1 = ("tT", v("tT"))
            if cc % 4 == 3:
                self.ring_prev[("r1", slot)] = t_mm1
            hw_ = [t_mm1]
            if cc >= 2:
                hw_.append(conv_done[cc - 2])
            elif self.conv_prev is not None:
                hw_.append(self.conv_prev[cc])
            evac_done[cc] = E("scalar", lambda Sc, sl=sl, bank=bank, c0=c0,
                              W=W: Sc.activation(
                hs[:, sl * FW + c0:sl * FW + c0 + W],
                pb[bank][:, c0:c0 + W], AF.Copy), waits=hw_, inc=("tS", 1))
            cwait = [evac_done[cc]]
            if cc >= 2:
                cwait.append(silu_done[cc - 2])
            elif self.cbuf_prev is not None:
                cwait.append(self.cbuf_prev[cc])
            cbo = sl * FT
            E("vector", lambda V, cc=cc, cbo=cbo, sl=sl: V.tensor_scalar(
                cbuf[:, cbo:cbo + FT], hs[:, sl * FW:sl * FW + FT],
                cw[:, cc * 3:cc * 3 + 1], cb[:, cc:cc + 1],
                ALU.mult, ALU.add), waits=cwait)
            E("vector", lambda V, cc=cc, cbo=cbo, sl=sl: V.scalar_tensor_tensor(
                cbuf[:, cbo:cbo + FT], hs[:, sl * FW + 1:sl * FW + FT + 1],
                cw[:, cc * 3 + 1:cc * 3 + 2], cbuf[:, cbo:cbo + FT],
                ALU.mult, ALU.add))
            vc = E("vector", lambda V, cc=cc, cbo=cbo, sl=sl:
                   V.scalar_tensor_tensor(
                cbuf[:, cbo:cbo + FT], hs[:, sl * FW + 2:sl * FW + FT + 2],
                cw[:, cc * 3 + 2:cc * 3 + 3], cbuf[:, cbo:cbo + FT],
                ALU.mult, ALU.add), inc=("tV", 1))
            conv_done[cc] = vc
            if pend_silu is not None:
                pcc, pvc = pend_silu
                sw_ = [pvc]
                if pcc in self.k2_prev:
                    sw_.append(self.k2_prev[pcc])
                silu_done[pcc] = E("scalar", lambda Sc, pcc=pcc: Sc.activation(
                    hp[:, pcc * FT:pcc * FT + FT],
                    cbuf[:, (pcc % 2) * FT:(pcc % 2) * FT + FT], AF.Silu),
                    waits=sw_, inc=("tS", 1))
            pend_silu = (cc, vc)
        pcc, pvc = pend_silu
        sw_ = [pvc]
        if pcc in self.k2_prev:
            sw_.append(self.k2_prev[pcc])
        silu_done[pcc] = E("scalar", lambda Sc, pcc=pcc: Sc.activation(
            hp[:, pcc * FT:pcc * FT + FT],
            cbuf[:, (pcc % 2) * FT:(pcc % 2) * FT + FT], AF.Silu),
            waits=sw_, inc=("tS", 1))
        self.s_silu = ("tS", v("tS"))
        self.cbuf_prev = (silu_done[NCC - 2], silu_done[NCC - 1])
        self.conv_prev = (conv_done[NCC - 2], conv_done[NCC - 1])
        self.evac_prev = (evac_done[NCC - 2], evac_done[NCC - 1])
        self.silu_done_t = silu_done

    def _F_quant2(self, t):
        E, v, pb, S = self.E, self.v, self.pb, self.S
        st = self.stats
        hp, k2i8, k2bf = S["hp"], S["k2i8"], S["k2bf"]
        mw2ch = S["mw2ch"]
        v_m2 = E("vector", lambda V: V.tensor_reduce(
            st["m2_b"][:, 0:FT],
            bass.AP(hp, 0, [[NCC * FT, 128], [1, FT], [FT, NCC]]),
            AX.X, ALU.max, apply_absolute_value=True),
            waits=[self.s_silu], inc=("tV", 1))
        self.repl_max(st["m2_b"], 0, FT, st["a2_b"], v_m2)
        E("vector", lambda V: V.tensor_scalar(
            st["a2_b"][:, 0:FT], st["a2_b"][:, 0:FT], 1e-5, None, ALU.max))
        vq2_r = E("vector", lambda V: V.reciprocal(
            st["m2_b"][:, 0:FT], st["a2_b"][:, 0:FT]), inc=("tV", 1))
        E("vector", lambda V: V.tensor_scalar(
            st["s2_b"][:, 0:FT], st["m2_b"][:, 0:FT], 127.0, None, ALU.mult),
            waits=[vq2_r])
        E("vector", lambda V: V.tensor_scalar(
            st["g2_b"][:, 0:FT], st["a2_b"][:, 0:FT], mw2ch[:, 0:1], None,
            ALU.mult), inc=("tV", 1))
        v_q2 = ("tV", v("tV"))
        if DBGS:
            dmark = None
            for j, nm in enumerate(("s2_b", "g2_b", "a2_b")):
                dmark = self.dma("scalar", "dC",
                                 self.dbg_q2[self.fidx, t, j, :],
                                 st[nm][0:1, 0:FT],
                                 waits=[v_q2] if j == 0 else [])
            E("vector", lambda V: V.engine_nop(), waits=[dmark])
        self.k2v = {}
        k2_cur = {}
        for cc in range(NCC):
            E("vector", lambda V, cc=cc: V.tensor_tensor(
                k2i8[:, cc * FT:cc * FT + FT], hp[:, cc * FT:cc * FT + FT],
                st["s2_b"][:, 0:FT], ALU.mult),
                waits=([v_q2] + ([self.mm2_mark] if self.mm2_mark else []))
                if cc == 0 else [], inc=("tV", 1))
            k2_cur[cc] = ("tV", v("tV"))
            E("scalar", lambda Sc, cc=cc: Sc.activation(
                k2bf[:, cc * FT:cc * FT + FT], k2i8[:, cc * FT:cc * FT + FT],
                AF.Copy), waits=[k2_cur[cc]], inc=("tS", 1))
            self.k2v[cc] = ("tS", v("tS"))
        self.k2_prev = k2_cur

    def _F_mm2_defer(self, seq, t, store_out):
        E, v, dma, pb, S = self.E, self.v, self.dma, self.pb, self.S
        st = self.stats
        k2bf, ring2 = S["k2bf"], S["ring2"]
        a0 = t * FT
        k2v = dict(self.k2v)
        xr_guard = self.q1_xr_guard

        def doit():
            for cc in range(NCC):
                g = cc // 4
                slot = g % 2
                if cc % 4 == 0:
                    self._ring_load(2, g, slot)
                waits = [k2v[cc]]
                if cc % 4 == 0:
                    waits.append(self.ring_marks[("r2", slot)])
                if cc == 0:
                    for bk in range(4):
                        waits += self.wbank(bk, [])
                for dm in range(NDC):
                    bank = dm // 2
                    half = (dm % 2) * FT
                    E("tensor", lambda T, cc=cc, dm=dm, bank=bank, half=half,
                      slot=slot: T.matmul(
                          pb[bank][:, half:half + FT],
                          ring2[:, slot * 4096 + (cc % 4) * 1024 + dm * 128:
                                slot * 4096 + (cc % 4) * 1024 + dm * 128 + 128],
                          k2bf[:, cc * FT:cc * FT + FT],
                          start=(cc == 0 and dm % 2 == 0),
                          stop=(cc == NCC - 1 and dm % 2 == 1)),
                      waits=waits if dm == 0 else [],
                      inc=("tT", 1) if (dm == NDC - 1 and cc % 4 == 3) else None)
                if cc % 4 == 3:
                    self.ring_prev[("r2", slot)] = ("tT", v("tT"))
            t_mm2 = ("tT", v("tT"))
            self.mm2_mark = t_mm2
            for bk in range(4):
                self.bank_ready[bk] = t_mm2
            for dm in range(NDC):
                bank = dm // 2
                half = (dm % 2) * FT
                ew = self.rbank(bank, []) if dm % 2 == 0 else []
                if dm == 0 and xr_guard is not None:
                    ew = ew + [xr_guard]
                E("vector", lambda V, bank=bank, half=half: V.tensor_tensor(
                    st["a2_b"][:, 0:FT], pb[bank][:, half:half + FT],
                    st["g2_b"][:, 0:FT], ALU.mult), waits=ew)
                E("vector", lambda V, dm=dm: V.tensor_tensor(
                    self._xr(dm, a0, FT), self._xr(dm, a0, FT),
                    st["a2_b"][:, 0:FT], ALU.add),
                    inc=("tV", 1) if dm == NDC - 1 else None)
            v_tile = ("tV", v("tV"))
            for bk in range(4):
                self.bank_free[bk] = v_tile
            if store_out:
                dstr = bass.AP(self.ofm, seq * DM * SEQ + a0,
                               [[SEQ, 128], [128 * SEQ, NDC], [1, FT]])
                srcr = bass.AP(S["xr"], a0,
                               [[NDC * SEQ, 128], [SEQ, NDC], [1, FT]])
                dma("scalar", "dC", dstr, srcr, waits=[v_tile])
        self.pending_mm2 = doit

    def _ring_load(self, which, g, slot):
        S, dma = self.S, self.dma
        key = ("r%d" % which, slot)
        waits = [self.scr_marks[(which, g % 8)]]
        if key in self.ring_prev:
            waits.append(self.ring_prev[key])
        eng, q = (("sync", "dS") if (g + which) % 2 == 0
                  else ("scalar", "dC"))
        if which == 1:
            src = bass.AP(self.t1s, (g % 8) * 128 * 4096,
                          [[4096, 128], [1, 4096]])
            dst = S["ring1"][:, slot * 4096:slot * 4096 + 4096]
        else:
            src = bass.AP(self.t2s, (g % 8) * 128 * 4096,
                          [[4096, 128], [1, 4096]])
            dst = S["ring2"][:, slot * 4096:slot * 4096 + 4096]
        self.ring_marks[key] = dma(eng, q, dst, src, waits=waits)

    def emit_A(self, seq):
        E, v, dma, pb, S = self.E, self.v, self.dma, self.pb, self.S
        ast = self.astats
        AST = self.AST
        at = lambda n: ast[n][:, 0:AST]
        qt_, kt_, vge, vgo = S["qt_"], S["kt_"], S["vge"], S["vgo"]
        osb, wqblk, vstage = S["osb"], S["wqblk"], S["vstage"]
        xnfull, ah16, aq16 = S["xnfull"], S["ah16"], S["aq16"]
        wo_sb = S["wo_sb"]
        ones16, onesm, onesbf, ident = (S["ones16"], S["onesm"], S["onesbf"],
                                        S["ident"])
        epsc, cneg, bq, bo = S["epsc"], S["cneg"], S["bq"], S["bo"]
        pte, pto, zrow, zhl, zbsb = (S["pte"], S["pto"], S["zrow"], S["zhl"],
                                     S["zbsb"])
        pte2, pto2 = S["pte2"], S["pto2"]
        pteo, pteo2 = S["pteo"], S["pteo2"]
        bw = self.barrier_waits()
        self.vstage_free = None
        self.pv_mark = {}
        self.pending_bcast = None

        for kc in range(NKC):
            for p in range(4):
                eo = (kc * 4 + p) * 65
                oo = (kc * 4 + p) * 128
                E("gpsimd", lambda g, eo=eo: g.memset(
                    vge[:, eo + 64:eo + 65], 1.0),
                    waits=bw if (kc == 0 and p == 0) else [])
                E("gpsimd", lambda g, oo=oo: g.memset(vgo[:, oo:oo + 1], 1.0))
                E("gpsimd", lambda g, oo=oo: g.memset(
                    vgo[:, oo + 1:oo + 64], 0.0))
        E("gpsimd", lambda g: g.memset(zrow[0:1, 0:1], 0.0), inc=("tG", 1))
        g_aug = ("tG", v("tG"))

        for t8 in range(SEQ // AST):
            t0 = t8 * AST
            for dc in range(NDC):
                E("vector", lambda V, dc=dc, t0=t0: V.tensor_copy(
                    ah16[:, dc * AST:dc * AST + AST], self._xr(dc, t0, AST)),
                    waits=bw if (t8 == 0 and dc == 0) else [])
            for dc in range(NDC):
                E("vector", lambda V, dc=dc: V.tensor_tensor(
                    aq16[:, dc * AST:dc * AST + AST],
                    ah16[:, dc * AST:dc * AST + AST],
                    ah16[:, dc * AST:dc * AST + AST], ALU.mult),
                    inc=("tV", 1) if dc == NDC - 1 else None)
            v16 = ("tV", v("tV"))
            for dc in range(NDC):
                E("tensor", lambda T, dc=dc: T.matmul(
                    pb[6][:, 0:AST], ones16[:, :], ah16[:, dc * AST:dc * AST + AST],
                    start=(dc == 0), stop=(dc == NDC - 1)),
                    waits=self.wbank(6, [v16]) if dc == 0 else [])
            for dc in range(NDC):
                E("tensor", lambda T, dc=dc: T.matmul(
                    pb[7][:, 0:AST], ones16[:, :], aq16[:, dc * AST:dc * AST + AST],
                    start=(dc == 0), stop=(dc == NDC - 1)),
                    waits=self.wbank(7, []) if dc == 0 else [],
                    inc=("tT", 1) if dc == NDC - 1 else None)
            tsum = ("tT", v("tT"))
            self.bank_ready[6] = tsum
            self.bank_ready[7] = tsum
            E("scalar", lambda Sc: Sc.activation(
                at("amean"), pb[6][:, 0:AST], AF.Copy, scale=1.0 / DM),
                waits=self.rbank(6, []))
            sm = E("scalar", lambda Sc: Sc.activation(
                at("atmp"), pb[7][:, 0:AST], AF.Copy, scale=1.0 / DM),
                waits=self.rbank(7, []), inc=("tS", 1))
            self.bank_free[6] = sm
            self.bank_free[7] = sm
            E("vector", lambda V: V.tensor_tensor(
                at("atmp2"), at("amean"), at("amean"), ALU.mult), waits=[sm])
            E("vector", lambda V: V.tensor_tensor(
                at("atmp"), at("atmp"), at("atmp2"), ALU.subtract))
            vv = E("vector", lambda V: V.tensor_scalar(
                at("atmp"), at("atmp"), 0.0, None, ALU.max), inc=("tV", 1))
            E("scalar", lambda Sc: Sc.activation(
                at("atmp2"), at("atmp"), AF.Ln, bias=epsc[:, 0:1]), waits=[vv])
            sr = E("scalar", lambda Sc: Sc.activation(
                at("arstd"), at("atmp2"), AF.Exp, scale=-0.5), inc=("tS", 1))
            for dc in range(NDC):
                E("vector", lambda V, dc=dc, t0=t0: V.tensor_tensor(
                    at("atmp2"), self._xr(dc, t0, AST), at("amean"),
                    ALU.subtract),
                    waits=[sr] if dc == 0 else [])
                E("vector", lambda V, dc=dc, t0=t0: V.tensor_tensor(
                    xnfull[:, dc * SEQ + t0:dc * SEQ + t0 + AST], at("atmp2"),
                    at("arstd"), ALU.mult),
                    inc=("tV", 1) if dc == NDC - 1 else None)
        v_xn = ("tV", v("tV"))
        E("gpsimd", lambda g: g.memset(zbsb[:, :], 1.0),
          waits=[v_xn], inc=("tG", 1))
        self.zbsb_claim = ("tG", v("tG"))

        wq_used = {0: None, 1: None}
        exp_done = {0: None, 1: None}
        for hg in range(2):
            fcs = ([hg * 4 + j for j in range(4)] +
                   [8 + hg * 4 + j for j in range(4)] +
                   [16 + hg * 4 + j for j in range(4)])
            for i, fc in enumerate(fcs):
                slot = i % 2
                srcw = bass.AP(self.wqkv_r, fc * DM * 128,
                               [[128, 128], [128 * 128, NDC], [1, 128]])
                lw = [v_xn] if (hg == 0 and i == 0) else []
                if wq_used[slot] is not None:
                    lw.append(wq_used[slot])
                vld = dma("gpsimd", "dG",
                          wqblk[:, slot * 1024:slot * 1024 + 1024], srcw,
                          waits=lw)
                role = "q" if fc < 8 else ("k" if fc < 16 else "v")
                lc = fc - (0 if fc < 8 else (8 if fc < 16 else 16)) - hg * 4
                for tt in range(NQT):
                    t0 = tt * QT
                    bank = 4 + ((i * NQT + tt) % 2)
                    waits = [vld, v_xn] if tt == 0 else []
                    waits = self.wbank(bank, waits)
                    for dc in range(NDC):
                        E("tensor", lambda T, dc=dc, slot=slot, bank=bank,
                          t0=t0: T.matmul(
                              pb[bank][:, 0:QT],
                              wqblk[:, slot * 1024 + dc * 128:
                                    slot * 1024 + dc * 128 + 128],
                              xnfull[:, dc * SEQ + t0:dc * SEQ + t0 + QT],
                              start=(dc == 0), stop=(dc == NDC - 1)),
                            waits=waits if dc == 0 else [],
                            inc=("tT", 1) if dc == NDC - 1 else None)
                    tq = ("tT", v("tT"))
                    self.bank_ready[bank] = tq
                    if tt == NQT - 1:
                        wq_used[slot] = tq
                    if role == "q":
                        dstp = qt_[:, lc * SEQ + t0:lc * SEQ + t0 + QT]
                    elif role == "k":
                        dstp = kt_[:, lc * SEQ + t0:lc * SEQ + t0 + QT]
                    else:
                        dstp = vstage[:, 0:QT]
                    evw = self.rbank(bank, [])
                    if role == "v" and self.vstage_free is not None:
                        evw.append(self.vstage_free)
                    sev = E("vector", lambda V, fc=fc, dstp=dstp, bank=bank:
                            V.tensor_scalar(dstp, pb[bank][:, 0:QT],
                                            bq[:, fc:fc + 1], None, ALU.add),
                            waits=evw, inc=("tV", 1))
                    self.bank_free[bank] = sev
                    if role == "v":
                        for tb in range(QT // 128):
                            kc = (t0 + tb * 128) // 128
                            tt2 = E("tensor", lambda T, tb=tb: T.transpose(
                                pb[7][:, 0:128],
                                vstage[:, tb * 128:tb * 128 + 128],
                                ident[:, :]),
                                waits=self.wbank(7, [sev, g_aug]
                                                 if tb == 0 else []),
                                inc=("tT", 1))
                            self.bank_ready[7] = tt2
                            E("vector", lambda V, kc=kc, lc=lc: V.tensor_copy(
                                vge[:, (kc * 4 + lc) * 65:(kc * 4 + lc) * 65 + 64],
                                pb[7][:, 0:64]), waits=self.rbank(7, []))
                            vt2 = E("vector", lambda V, kc=kc, lc=lc:
                                    V.tensor_copy(
                                        vgo[:, (kc * 4 + lc) * 128 + 64:
                                            (kc * 4 + lc) * 128 + 128],
                                        pb[7][:, 64:128]), inc=("tV", 1))
                            self.bank_free[7] = vt2
                            if tb == QT // 128 - 1:
                                self.vstage_free = tt2
            v_sweep = ("tV", v("tV"))
            s_sweep = ("tS", v("tS"))


            def emit_pv(prev, p):
                se, so, kc = prev
                par = kc % 2
                wpe = self.wbank(2, [se]) if kc == 0 else [se]
                E("tensor", lambda T, p=p, kc=kc, par=par: T.matmul(
                    pb[2][0:65, 0:QT],
                    vge[:, (kc * 4 + p) * 65:(kc * 4 + p) * 65 + 65],
                    (pteo if par == 0 else pteo2)[:, 0:QT],
                    start=(kc == 0), stop=(kc == NKC - 1)),
                    waits=wpe)
                wpo = self.wbank(3, [so]) if kc == 0 else [so]
                E("tensor", lambda T, p=p, kc=kc, par=par: T.matmul(
                    pb[3][0:128, 0:QT],
                    vgo[:, (kc * 4 + p) * 128:(kc * 4 + p) * 128 + 128],
                    (pteo if par == 0 else pteo2)[:, QT:2 * QT],
                    start=(kc == 0), stop=(kc == NKC - 1)),
                    waits=wpo, inc=("tT", 1))
                self.pv_mark[par] = ("tT", v("tT"))
            for qt in range(NQT):
                q0 = qt * QT
                for p in range(4):
                    gp = hg * 4 + p
                    prev = None
                    for kc in range(NKC):
                        par = kc % 2
                        waits = self.wbank(0, [exp_done[0]]
                                           if exp_done[0] else [])
                        if qt == 0 and p == 0 and kc == 0:
                            waits += [v_sweep, s_sweep]
                        E("tensor", lambda T, p=p, kc=kc, q0=q0: T.matmul(
                            pb[0][:, 0:QT],
                            kt_[0:64, p * SEQ + kc * 128:p * SEQ + kc * 128 + 128],
                            qt_[0:64, p * SEQ + q0:p * SEQ + q0 + QT],
                            start=True, stop=True),
                            waits=waits, inc=("tT", 1))
                        E("tensor", lambda T, p=p, kc=kc, q0=q0: T.matmul(
                            pb[1][:, 0:QT],
                            kt_[64:128, p * SEQ + kc * 128:
                                p * SEQ + kc * 128 + 128],
                            qt_[64:128, p * SEQ + q0:p * SEQ + q0 + QT],
                            start=True, stop=True),
                            inc=("tT", 1))
                        tso = ("tT", v("tT"))
                        ow = [tso]
                        if self.pv_mark.get(par) is not None:
                            ow.append(self.pv_mark[par])
                        so = E("scalar", lambda Sc, par=par: Sc.activation(
                            (pteo if par == 0 else pteo2)[:, :],
                            self.pb01[:, 0:2 * QT], AF.Exp, scale=0.125,
                            bias=cneg[:, 0:1]), waits=ow, inc=("tS", 1))
                        exp_done[0] = so
                        exp_done[1] = so
                        self.bank_free[0] = so
                        self.bank_free[1] = so
                        if prev is not None:
                            emit_pv(prev, p)
                        if kc == 6 and self.pending_bcast is not None:
                            self.pending_bcast()
                            self.pending_bcast = None
                        prev = (so, so, kc)
                    emit_pv(prev, p)
                    t_pv = self.pv_mark[(NKC - 1) % 2]
                    self.bank_ready[2] = t_pv
                    self.bank_ready[3] = t_pv
                    stw = self.rbank(2, self.rbank(3, []))
                    stw += self.wbank(5, self.wbank(6, []))
                    E("vector", lambda V: V.tensor_copy(
                        pb[5][0:65, 0:QT], pb[2][0:65, 0:QT]), waits=stw)
                    vst = E("vector", lambda V: V.tensor_copy(
                        pb[6][:, 0:QT], pb[3][:, 0:QT]), inc=("tV", 1))
                    self.bank_free[2] = vst
                    self.bank_free[3] = vst
                    zcw = []
                    if self.zbsb_claim is not None:
                        zcw.append(self.zbsb_claim)
                        self.zbsb_claim = None
                    E("vector", lambda V: V.tensor_copy(
                        zbsb[64:65, 0:QT], pb[5][64:65, 0:QT]), waits=zcw)
                    E("vector", lambda V: V.tensor_copy(
                        zbsb[0:1, 0:QT], pb[6][0:1, 0:QT]))
                    E("vector", lambda V: V.reciprocal(
                        zrow[0:65, 0:QT], zbsb[0:65, 0:QT]))
                    E("vector", lambda V: V.tensor_copy(
                        zhl[64:65, 0:QT], zrow[64:65, 0:QT]))
                    E("vector", lambda V: V.tensor_tensor(
                        zhl[64:65, QT:2 * QT], zrow[64:65, 0:QT],
                        zhl[64:65, 0:QT], ALU.subtract))
                    E("vector", lambda V: V.tensor_copy(
                        zhl[0:1, 0:QT], zrow[0:1, 0:QT]))
                    vz = E("vector", lambda V: V.tensor_tensor(
                        zhl[0:1, QT:2 * QT], zrow[0:1, 0:QT],
                        zhl[0:1, 0:QT], ALU.subtract), inc=("tV", 1))

                    ob = (qt * 8 + gp) * QT

                    def tail_bcast(vz=vz, ob=ob):
                        E("tensor", lambda T: T.matmul(
                            pb[4][0:64, 0:QT], onesbf[64:65, 0:64],
                            zhl[64:65, 0:QT], start=True, stop=False),
                            waits=self.wbank(4, [vz]))
                        E("tensor", lambda T: T.matmul(
                            pb[4][0:64, 0:QT], onesbf[64:65, 0:64],
                            zhl[64:65, QT:2 * QT], start=False, stop=True))
                        E("tensor", lambda T: T.matmul(
                            pb[4][64:128, 0:QT], onesbf[0:1, 0:64],
                            zhl[0:1, 0:QT], start=True, stop=False))
                        tz = E("tensor", lambda T: T.matmul(
                            pb[4][64:128, 0:QT], onesbf[0:1, 0:64],
                            zhl[0:1, QT:2 * QT], start=False, stop=True),
                            inc=("tT", 1))
                        self.bank_ready[4] = tz
                        vzb = E("vector", lambda V: V.tensor_copy(
                            zbsb[:, 0:QT], pb[4][:, 0:QT]),
                            waits=self.rbank(4, []), inc=("tV", 1))
                        self.bank_free[4] = vzb
                        E("vector", lambda V: V.tensor_tensor(
                            osb[0:64, ob:ob + QT], pb[5][0:64, 0:QT],
                            zbsb[0:64, 0:QT], ALU.mult))
                        vo = E("vector", lambda V: V.tensor_tensor(
                            osb[64:128, ob:ob + QT], pb[6][64:128, 0:QT],
                            zbsb[64:128, 0:QT], ALU.mult), inc=("tV", 1))
                        self.bank_free[5] = vo
                        self.bank_free[6] = vo
                    self.pending_bcast = tail_bcast
            if self.pending_bcast is not None:
                self.pending_bcast()
                self.pending_bcast = None
        v_core = ("tV", v("tV"))

        srcw = bass.AP(self.wout, 0, [[DM, 128], [128 * DM, NDC], [1, DM]])
        vwo = dma("gpsimd", "dG", wo_sb[:, :],
                  srcw, waits=[("tT", v("tT")), v_core])
        for qt in range(NQT):
            q0 = qt * QT
            for dm in range(NDC):
                bank = 2 + (dm % 2)
                waits = [v_core, vwo] if (qt == 0 and dm == 0) else []
                waits = self.wbank(bank, waits)
                for dc in range(NDC):
                    E("tensor", lambda T, dm=dm, dc=dc, bank=bank, qt=qt:
                      T.matmul(
                          pb[bank][:, 0:QT],
                          wo_sb[:, dc * DM + dm * 128:dc * DM + dm * 128 + 128],
                          osb[:, (qt * 8 + dc) * QT:(qt * 8 + dc) * QT + QT],
                          start=(dc == 0), stop=(dc == NDC - 1)),
                        waits=waits if dc == 0 else [],
                        inc=("tT", 1) if dc == NDC - 1 else None)
                to = ("tT", v("tT"))
                self.bank_ready[bank] = to
                vo2 = E("vector", lambda V, dm=dm, bank=bank, q0=q0:
                        V.scalar_tensor_tensor(
                            self._xr(dm, q0, QT), pb[bank][:, 0:QT],
                            bo[:, dm:dm + 1], self._xr(dm, q0, QT),
                            ALU.add, ALU.add),
                        waits=self.rbank(bank, []),
                        inc=("tV", 1))
                self.bank_free[bank] = vo2


    def finalize(self):
        nc = self.nc
        sems = {}
        import contextlib
        stack = contextlib.ExitStack()
        for sname in ("dS", "dC", "dG", "tT", "tV", "tS", "tG"):
            sems[sname] = stack.enter_context(nc.semaphore(sname))
        block = stack.enter_context(nc.Block())

        def runner(eng_name):
            def run(eng):
                for waits, build, inc in self.ops[eng_name]:
                    for (sn, val) in waits:
                        eng.wait_ge(sems[sn], val)
                    instr = build(eng)
                    if inc is not None:
                        assert instr is not None, (eng_name, inc)
                        instr.then_inc(sems[inc[0]], inc[1])
            return run

        block.sync(runner("sync"))
        block.tensor(runner("tensor"))
        block.vector(runner("vector"))
        block.scalar(runner("scalar"))
        block.gpsimd(runner("gpsimd"))
        stack.close()
        return nc


def build_nc():
    b = Builder()
    return b.finalize()


_NC_CACHE = {}


def kernel(L0_attn_x, L0_attn_w_qkv, L0_attn_b_qkv, L0_attn_w_out, L0_attn_b_out,
           L0_ffn_w1, L0_ffn_conv_w, L0_ffn_conv_b, L0_ffn_w2):
    x = np.asarray(L0_attn_x, dtype=np.float32)
    wqkv = np.asarray(L0_attn_w_qkv, dtype=np.float32)
    wqkv_r = np.ascontiguousarray(wqkv.reshape(DM, 24, 128).transpose(1, 0, 2))
    bqkv = np.asarray(L0_attn_b_qkv, dtype=np.float32)
    wout = np.ascontiguousarray(np.asarray(L0_attn_w_out, dtype=np.float32))
    bout = np.asarray(L0_attn_b_out, dtype=np.float32)
    w1 = np.asarray(L0_ffn_w1, dtype=np.float32)
    convw = np.asarray(L0_ffn_conv_w, dtype=np.float32).reshape(DH, 3)
    convb = np.asarray(L0_ffn_conv_b, dtype=np.float32)
    w2 = np.asarray(L0_ffn_w2, dtype=np.float32)

    w1t = np.ascontiguousarray(w1.T)
    w2t = np.ascontiguousarray(w2.T)
    bqr = np.ascontiguousarray(bqkv.reshape(24, 128).T)
    bor = np.ascontiguousarray(bout.reshape(8, 128).T)
    cwr = np.ascontiguousarray(
        convw.reshape(32, 128, 3).transpose(1, 0, 2).reshape(128, 96))
    cbr = np.ascontiguousarray(convb.reshape(32, 128).T)

    if "nc" not in _NC_CACHE:
        _NC_CACHE["nc"] = build_nc()
    nc = _NC_CACHE["nc"]

    in_maps = []
    for c in range(8):
        xfm = np.ascontiguousarray(x[2 * c:2 * c + 2].transpose(0, 2, 1))
        in_maps.append({
            "xfm": xfm, "wqkv_r": wqkv_r, "wout": wout, "w1t": w1t, "w2t": w2t,
            "bqr": bqr, "bor": bor, "cwr": cwr, "cbr": cbr,
        })
    res = run_bass_kernel_spmd(nc, in_maps, core_ids=list(range(8)))
    out = np.empty((16, SEQ, DM), dtype=np.float32)
    for c in range(8):
        out[2 * c:2 * c + 2] = res.results[c]["ofm"].transpose(0, 2, 1)
    return out


# version 15
# speedup vs baseline: 1.1469x; 1.1469x over previous
import numpy as np
import ml_dtypes
import concourse.bass as bass
import concourse.mybir as mybir
from concourse.bass_utils import run_bass_kernel_spmd

F32 = mybir.dt.float32
F16 = mybir.dt.float16
BF16 = mybir.dt.bfloat16
I8 = mybir.dt.int8
AF = mybir.ActivationFunctionType
ALU = mybir.AluOpType
AX = mybir.AxisListType

SEQ = 2048
DM = 1024
DH = 4096
NDC = 8
NCC = 32
FT = 256
FW = FT + 2
NFT = SEQ // FT
QT = 512
NQT = SEQ // QT
NKC = SEQ // 128
CSHIFT = -24.0
PB = 512

ENGS = ("sync", "tensor", "vector", "scalar", "gpsimd")
PHASES = "full"
DEBUG = False
DBGS = False
DBGS_Q1 = True


class Builder:
    def __init__(self):
        self.nc = bass.Bass(target_bir_lowering=False,
                            detect_race_conditions=False)
        self.ops = {e: [] for e in ENGS}
        self.cnt = {}
        self.bank_ready = {}
        self.bank_free = {}
        self._build()

    def v(self, s):
        return self.cnt.get(s, 0)

    def E(self, eng, build, waits=(), inc=None):
        waits = [w for w in waits if w is not None]
        if inc is not None:
            self.cnt[inc[0]] = self.cnt.get(inc[0], 0) + inc[1]
        self.ops[eng].append((tuple(waits), build, inc))
        return None if inc is None else (inc[0], self.cnt[inc[0]])

    def dma(self, eng, sem, dst, src, waits=(), slow=False):
        def b(e):
            if slow:
                return e.dma_start(dst, src, allow_slow_non_contiguous=True)
            return e.dma_start(dst, src)
        self.E(eng, b, waits, (sem, 16))
        return (sem, self.v(sem))

    def wbank(self, i, waits):
        w = list(waits)
        if i in self.bank_free:
            w.append(self.bank_free[i])
        return w

    def rbank(self, i, waits):
        w = list(waits)
        if i in self.bank_ready:
            w.append(self.bank_ready[i])
        return w

    def vflush(self, waits=(), inc=None):
        S = self.S
        return self.E("vector", lambda V: V.tensor_copy(
            S["junkf"][:, 0:128], S["ident"][:, :]), waits, inc)

    def barrier_waits(self):
        return [(s, self.v(s)) for s in
                ("tT", "tV", "tS", "tG", "dS", "dC", "dG") if self.v(s) > 0]

    def phase_barrier(self):
        bw = self.barrier_waits()
        for e in ENGS:
            self.E(e, lambda eng: eng.nop(), bw)

    def _build(self):
        nc = self.nc
        E, dma, v = self.E, self.dma, self.v

        self.xfm = nc.dram_tensor("xfm", [2, DM, SEQ], F32, kind="ExternalInput")
        self.wqkv_r = nc.dram_tensor("wqkv_r", [24, DM, 128], F32,
                                     kind="ExternalInput")
        self.wout = nc.dram_tensor("wout", [DM, DM], F32, kind="ExternalInput")
        self.w1t = nc.dram_tensor("w1t", [DM, DH], F32, kind="ExternalInput")
        self.w2t = nc.dram_tensor("w2t", [DH, DM], F32, kind="ExternalInput")
        self.bqr = nc.dram_tensor("bqr", [128, 24], F32, kind="ExternalInput")
        self.bor = nc.dram_tensor("bor", [128, 8], F32, kind="ExternalInput")
        self.cwr = nc.dram_tensor("cwr", [128, 96], F32, kind="ExternalInput")
        self.cbr = nc.dram_tensor("cbr", [128, 32], F32, kind="ExternalInput")
        self.ofm = nc.dram_tensor("ofm", [2, DM, SEQ], F32, kind="ExternalOutput")
        self.t1s = nc.dram_tensor("t1s", [8, 128, 4096], BF16, kind="Internal")
        if DBGS:
            self.dbg_q1 = nc.dram_tensor("dbg_q1", [4, 4, 5, 512], F32,
                                         kind="ExternalOutput")
            self.dbg_q2 = nc.dram_tensor("dbg_q2", [4, 8, 3, 256], F32,
                                         kind="ExternalOutput")
        self.t2s = nc.dram_tensor("t2s", [8, 128, 4096], BF16, kind="Internal")
        if DEBUG:
            self.dbg_st = nc.dram_tensor("dbg_st", [128, 2120], F32,
                                         kind="ExternalOutput")

        S = {}

        def sb(name, shape, dtype, off):
            t = nc.alloc_sbuf_tensor_at(name, shape, dtype, offset=off)
            S[name] = t
            return t
        self.S = S

        C = 18432
        ident = sb("ident", [128, 128], F32, C + 0)
        onesm = sb("onesm", [128, 128], F32, C + 512)
        ones16 = sb("ones16", [128, 128], F16, C + 1024)
        onesbf = sb("onesbf", [128, 128], BF16, C + 1280)
        cneg = sb("cneg", [128, 1], F32, C + 1536)
        epsc = sb("epsc", [128, 1], F32, C + 1568)
        bq = sb("bq", [128, 24], F32, C + 1600)
        bo = sb("bo", [128, 8], F32, C + 1696)
        cw = sb("cw", [128, 96], F32, C + 1728)
        cb = sb("cb", [128, 32], F32, C + 2112)
        mw1c = sb("mw1c", [128, 1], F32, C + 2240)
        mw2ch = sb("mw2ch", [128, 1], F32, C + 2272)
        sw1 = sb("sw1", [128, 1], F32, C + 2304)
        sw2 = sb("sw2", [128, 1], F32, C + 2336)
        wacc = sb("wacc", [128, 16], F32, C + 2368)
        xsave = sb("xsave", [128, 8], F32, C + 2432)
        wstat = sb("wstat", [128, 8], F32, C + 2496)
        xr = sb("xr", [128, NDC * SEQ], F32, C + 2560)

        D = C + 2560 + 65536
        Q1W = 512
        NQ1C = SEQ // Q1W
        ring1 = sb("ring1", [128, 2 * 4096], BF16, D)
        ring2 = sb("ring2", [128, 2 * 4096], BF16, D + 16384)
        k1full = sb("k1full", [128, NDC * SEQ], BF16, D + 32768)
        hs = sb("hs", [128, 2 * FW], F32, D + 65536)
        cbuf = sb("cbuf", [128, 2 * FT], F32, D + 67616)
        hp = sb("hp", [128, NCC * FT], F32, D + 69664)
        k2i8 = sb("k2i8", [128, NCC * FT], I8, D + 102432)
        k2bf = sb("k2bf", [128, NCC * FT], BF16, D + 110624)
        self.q1stats = {}
        off = D + 69664
        for sn in ("mean_c", "tmp_c", "tmp2_c", "rstd_c", "m1_c", "a_c",
                   "q_c", "g1_c"):
            self.q1stats[sn] = sb(sn, [128, Q1W], F32, off)
            off += 2048
        xcq = sb("xcq", [128, NDC * Q1W], F32, D + 86048)
        k1i8q = sb("k1i8q", [128, NDC * Q1W], I8, D + 102432)
        xh16q = sb("xh16q", [128, NDC * Q1W], F16, D + 110624)
        xq16q = sb("xq16q", [128, NDC * Q1W], F16, D + 118816)
        self.stats = {}
        self.stats["m2_b"] = sb("m2_b", [128, FT], F32, D + 127104)
        self.stats["a2_b"] = sb("a2_b", [128, FT], F32, D + 128128)
        self.stats["s2_b"] = sb("s2_b", [128, FT], F32, D + 139936)
        self.stats["g2_b"] = sb("g2_b", [128, FT], F32, D + 140960)
        junkf = sb("junkf", [128, 128], F32, D + 141984)
        a_tm = sb("a_tm", [128, 4], F32, D + 137344)
        aT = sb("aT", [4, 128], F32, D + 137376)
        arow = sb("arow", [1, 512], F32, D + 137888)
        self.Q1W, self.NQ1C = Q1W, NQ1C
        assert D + 142496 <= 229344

        xnfull = sb("xnfull", [128, NDC * SEQ], BF16, D)
        qt_ = sb("qt_", [128, 4 * SEQ], BF16, D + 32768)
        kt_ = sb("kt_", [128, 4 * SEQ], BF16, D + 49152)
        vge = sb("vge", [128, NKC * 4 * 65], BF16, D + 65536)
        vgo = sb("vgo", [128, NKC * 4 * 128], BF16, D + 73856)
        osb = sb("osb", [128, NQT * 8 * QT], BF16, D + 90240)
        wqblk = sb("wqblk", [128, 2 * NDC * 128], BF16, D + 123008)
        vstage = sb("vstage", [128, QT], F32, D + 127104)
        AST = 256
        ah16 = sb("ah16", [128, NDC * AST], F16, D + 129152)
        aq16 = sb("aq16", [128, NDC * AST], F16, D + 133248)
        self.astats = {}
        off = D + 137344
        for sn in ("amean", "atmp", "atmp2", "arstd"):
            self.astats[sn] = sb(sn, [128, AST], F32, off)
            off += 1024
        assert off <= 229344, off
        pte = sb("pte", [128, QT], BF16, D + 129152)
        pto = sb("pto", [128, QT], BF16, D + 130176)
        zhl = sb("zhl", [128, 2 * QT], BF16, D + 131200)
        pte2 = sb("pte2", [128, QT], BF16, D + 133248)
        pto2 = sb("pto2", [128, QT], BF16, D + 134272)
        pteo = sb("pteo", [128, 2 * QT], BF16, D + 129152)
        pteo2 = sb("pteo2", [128, 2 * QT], BF16, D + 133248)
        zrow = sb("zrow", [128, QT], F32, D + 137344)
        zbsb = sb("zbsb", [128, QT], F32, D + 139392)
        wo_sb = sb("wo_sb", [128, NDC * DM], BF16, D + 65536)
        self.AST = AST
        wqf = sb("wqf", [128, 8 * 1024], F32, D)
        wqf2 = sb("wqf2", [128, 8 * 1024], F32, D + 69664)
        wqb = sb("wqb", [128, 8 * 1024], BF16, D + 32768)
        wqi = sb("wqi", [128, 8 * 1024], I8, D + 49152)

        pb01 = nc.alloc_psum_tensor("pb01", [128, 2 * PB], F32)

        class _BankView:
            def __init__(self, t, off):
                self.t, self.off = t, off

            def __getitem__(self, idx):
                rows, cols = idx
                return self.t[rows, cols.start + self.off:cols.stop + self.off]
        pb = ([_BankView(pb01, 0), _BankView(pb01, PB)] +
              [nc.alloc_psum_tensor("pb%d" % i, [128, PB], F32)
               for i in range(2, 8)])
        self.pb01 = pb01
        self.pb = pb

        mi = E("gpsimd", lambda g: g.memset(ident[:, :], 0.0), inc=("tG", 1))
        E("gpsimd", lambda g: g.affine_select(
            out=ident[:, :], in_=ident[:, :], compare_op=ALU.not_equal,
            fill=1.0, base=0, pattern=[[-1, 128]], channel_multiplier=1),
            waits=[mi])
        E("gpsimd", lambda g: g.memset(onesm[:, :], 1.0))
        E("gpsimd", lambda g: g.memset(ones16[:, :], 1.0))
        E("gpsimd", lambda g: g.memset(onesbf[:, :], 1.0))
        E("gpsimd", lambda g: g.memset(cneg[:, :], CSHIFT))
        E("gpsimd", lambda g: g.memset(epsc[:, :], 1e-5))
        E("gpsimd", lambda g: g.memset(wacc[:, :], 0.0))
        E("gpsimd", lambda g: g.memset(a_tm[:, :], 0.0))
        E("gpsimd", lambda g: g.memset(aT[:, :], 0.0))
        E("gpsimd", lambda g: g.memset(arow[:, :], 0.0))
        E("gpsimd", lambda g: g.memset(wstat[:, :], 0.0))
        E("gpsimd", lambda g: g.memset(junkf[:, :], 0.0), inc=("tG", 1))
        gp_const = ("tG", v("tG"))
        dma("scalar", "dC", bq[:, :], self.bqr[:, :])
        dma("scalar", "dC", bo[:, :], self.bor[:, :])
        dma("scalar", "dC", cw[:, :], self.cwr[:, :])
        w_bias = dma("scalar", "dC", cb[:, :], self.cbr[:, :])

        wqf2 = S["wqf2"]
        bufs = (wqf, wqf2)
        queues = (("sync", "dS"), ("gpsimd", "dG"))

        def load_w(i, waits=()):
            wi, g = divmod(i, 4)
            if wi == 0:
                src = bass.AP(self.w1t, g * 1024,
                              [[DH, 128], [128 * DH, 8], [1, 1024]])
            else:
                src = bass.AP(self.w2t, g * 1024 * DM,
                              [[DM, 128], [128 * DM, 8], [1, 1024]])
            eng, q = queues[i % 2]
            return dma(eng, q, bufs[i % 2][:, :], src, waits=waits)

        red_marks = {}
        for i in range(8):
            vl = load_w(i, waits=[red_marks[i - 2]] if i >= 2 else [])
            red_marks[i] = E("vector", lambda V, i=i: V.tensor_reduce(
                wacc[:, i:i + 1], bufs[i % 2][:, :], AX.X,
                ALU.add, apply_absolute_value=True),
                waits=[vl, gp_const], inc=("tV", 1))
        self.vflush()
        E("vector", lambda V: V.memset(wqb[:, 0:256], 0.0))
        E("vector", lambda V: V.tensor_copy(wqb[:, 0:8], wacc[:, 0:8]))
        self.vflush()
        E("vector", lambda V: V.tensor_tensor(
            wqb[:, 128:136], wacc[:, 0:8], wqb[:, 0:8], ALU.subtract))
        v_hl = self.vflush(inc=("tV", 1))
        E("tensor", lambda T: T.matmul(
            pb[7][:, 0:128], onesbf[:, :], wqb[:, 0:128], start=True, stop=False),
            waits=self.wbank(7, [v_hl]))
        wt2 = E("tensor", lambda T: T.matmul(
            pb[7][:, 0:128], onesbf[:, :], wqb[:, 128:256], start=False, stop=True),
            inc=("tT", 1))
        self.bank_ready[7] = wt2
        E("vector", lambda V: V.tensor_copy(wacc[:, 8:16], pb[7][:, 0:8]),
            waits=self.rbank(7, []))
        vfr = self.vflush(inc=("tV", 1))
        self.bank_free[7] = vfr
        E("vector", lambda V: V.tensor_reduce(
            wstat[:, 0:1], wacc[:, 8:12], AX.X, ALU.add))
        self.vflush()
        E("vector", lambda V: V.tensor_reduce(
            wstat[:, 1:2], wacc[:, 12:16], AX.X, ALU.add))
        self.vflush()
        E("vector", lambda V: V.tensor_scalar(
            wstat[:, 2:4], wstat[:, 0:2], 1.0 / (DM * DH), None, ALU.mult))
        self.vflush()
        E("vector", lambda V: V.tensor_scalar(
            wstat[:, 4:6], wstat[:, 2:4], 1e-5, None, ALU.max))
        self.vflush()
        E("vector", lambda V: V.tensor_scalar(
            mw1c[:, 0:1], wstat[:, 4:5], 1.0 / 127.0, None, ALU.mult))
        E("vector", lambda V: V.tensor_scalar(
            mw2ch[:, 0:1], wstat[:, 5:6], 0.5 / 127.0, None, ALU.mult))
        self.vflush()
        E("vector", lambda V: V.reciprocal(sw1[:, 0:1], wstat[:, 4:5]))
        self.vflush()
        E("vector", lambda V: V.reciprocal(sw2[:, 0:1], wstat[:, 5:6]))
        v_repl = self.vflush(inc=("tV", 1))

        cast_marks = {}
        prev_store = None
        self.scr_marks = {}
        for i in range(8):
            swc = sw1 if i < 4 else sw2
            if i >= 2:
                lw = [cast_marks[i - 2]]
            else:
                lw = [red_marks[6 + (i % 2)]]
            vl = load_w(i, waits=lw)
            buf = bufs[i % 2]
            E("vector", lambda V, swc=swc, buf=buf: V.tensor_scalar(
                buf[:, :], buf[:, :], swc[:, 0:1], 1.4, ALU.mult, ALU.min),
                waits=[vl, v_repl])
            vclip = E("vector", lambda V, buf=buf: V.tensor_scalar(
                buf[:, :], buf[:, :], -1.4, None, ALU.max), inc=("tV", 1))
            cast_marks[i] = E("scalar", lambda Sc, buf=buf: Sc.activation(
                wqi[:, :], buf[:, :], AF.Copy), waits=[vclip], inc=("tS", 1))
            vb = E("vector", lambda V: V.tensor_copy(wqb[:, :], wqi[:, :]),
                   waits=[cast_marks[i]] + ([prev_store] if prev_store else []),
                   inc=("tV", 1))
            wi, g = divmod(i, 4)
            if wi == 0:
                for u in range(2):
                    dst = bass.AP(self.t1s, (2 * g + u) * 128 * 4096,
                                  [[4096, 128], [512, 8], [1, 512]])
                    srcb = bass.AP(wqb, u * 512,
                                   [[8192, 128], [1024, 8], [1, 512]])
                    prev_store = dma("scalar", "dC", dst, srcb, waits=[vb])
                    self.scr_marks[(1, 2 * g + u)] = prev_store
            else:
                dst = bass.AP(self.t2s, (2 * g) * 128 * 4096,
                              [[4096, 128], [128 * 4096, 2], [1, 4096]])
                prev_store = dma("scalar", "dC", dst, wqb[:, :], waits=[vb])
                self.scr_marks[(2, 2 * g)] = prev_store
                self.scr_marks[(2, 2 * g + 1)] = prev_store
        self.dc_scr = ("dC", v("dC"))
        self.ring_prev = {}
        self.ring_marks = {}

        for seq in (0, 1):
            self.phase_barrier()
            self.emit_F(seq, first=True, store_out=(PHASES == "f1"))
            if PHASES == "f1":
                continue
            self.phase_barrier()
            self.emit_A(seq)
            if PHASES == "f1a":
                self.phase_barrier()
                dstr = bass.AP(self.ofm, seq * DM * SEQ,
                               [[SEQ, 128], [128 * SEQ, NDC], [1, SEQ]])
                dma("scalar", "dC", dstr, S["xr"][:, :],
                    waits=self.barrier_waits())
                continue
            self.phase_barrier()
            self.emit_F(seq, first=False, store_out=True)

        if DEBUG:
            bwf = self.barrier_waits()
            dma("scalar", "dC", self.dbg_st[:, :], S["dbgstage"][:, :], waits=bwf)
        fin = [("dC", v("dC")), ("dS", v("dS")), ("dG", v("dG"))]
        for e in ENGS:
            self.E(e, lambda eng: eng.nop(), fin)

    def repl_max(self, m_src, c0, W, dst_b, v_wait):
        E, v, pb = self.E, self.v, self.pb
        S = self.S
        ident, onesm = S["ident"], S["onesm"]
        a_tm, aT, arow = S["a_tm"], S["aT"], S["arow"]
        nblk = (W + 127) // 128
        for b in range(nblk):
            cc0 = c0 + b * 128
            cwd = min(128, c0 + W - cc0)
            bk = 6 + (b % 2)
            tb = E("tensor", lambda T, cc0=cc0, cwd=cwd, bk=bk: T.transpose(
                pb[bk][0:cwd, 0:128],
                m_src[:, cc0:cc0 + cwd], ident[:, :]),
                waits=self.wbank(bk, [v_wait] if b == 0 else []),
                inc=("tT", 1))
            self.bank_ready[bk] = tb
            vr = E("vector", lambda V, cwd=cwd, b=b, bk=bk: V.tensor_reduce(
                a_tm[0:cwd, b:b + 1], pb[bk][0:cwd, 0:128],
                AX.X, ALU.max),
                waits=self.rbank(bk, []), inc=("tV", 1))
            self.bank_free[bk] = vr
            self.vflush()
        v_red = self.vflush(inc=("tV", 1))
        t_at = E("tensor", lambda T, nblk=nblk: T.transpose(
            pb[7][0:nblk, 0:128], a_tm[:, 0:nblk], ident[:, :]),
            waits=self.wbank(7, [v_red]), inc=("tT", 1))
        self.bank_ready[7] = t_at
        E("vector", lambda V, nblk=nblk: V.tensor_copy(
            aT[0:nblk, :], pb[7][0:nblk, 0:128]),
            waits=self.rbank(7, []))
        v_at = self.vflush(inc=("tV", 1))
        self.bank_free[7] = v_at
        full = (W // 128) * 128
        g1 = g2 = None
        if full > 0:
            dsta = bass.AP(arow, c0, [[FW, 1], [128, W // 128], [1, 128]])
            g1 = self.dma("gpsimd", "dG", dsta, aT[0:W // 128, :], waits=[v_at])
        rem = W - full
        if rem > 0:
            dsta = bass.AP(arow, c0 + full, [[FW, 1], [1, rem]])
            g2 = self.dma("gpsimd", "dG", dsta, aT[W // 128:W // 128 + 1, 0:rem],
                          waits=[v_at] if full == 0 else [])
        dg = ("dG", v("dG"))
        t_rep = E("tensor", lambda T, c0=c0, W=W: T.matmul(
            pb[7][:, 0:W], onesm[0:1, :], arow[:, c0:c0 + W],
            start=True, stop=True),
            waits=self.wbank(7, [dg, v_at]), inc=("tT", 1))
        self.bank_ready[7] = t_rep
        v_ab = E("vector", lambda V, c0=c0, W=W, dst_b=dst_b: V.tensor_copy(
            dst_b[:, c0:c0 + W], pb[7][:, 0:W]),
            waits=self.rbank(7, []), inc=("tV", 1))
        self.bank_free[7] = v_ab
        return v_ab

    def emit_F(self, seq, first, store_out):
        E, v, dma, pb, S = self.E, self.v, self.dma, self.pb, self.S
        vx = None
        if first:
            src = bass.AP(self.xfm, seq * DM * SEQ,
                          [[SEQ, 128], [128 * SEQ, NDC], [1, SEQ]])
            bwx = self.barrier_waits() if seq else []
            vx = dma("scalar", "dC", S["xr"][:, :], src, waits=bwx)
        self.q1_marks = {}
        self.q1_xr_guard = None
        self.pending_mm2 = None
        self.cbuf_prev = None
        self.conv_prev = None
        self.evac_prev = None
        self.k2_prev = {}
        self.mm2_mark = None
        self.fidx = getattr(self, 'fidx', -1) + 1
        for ch in range(self.NQ1C):
            self.emit_Q1(ch, vx if ch == 0 else None)
        for t in range(NFT):
            self._F_mm1(seq, t)
            if self.pending_mm2 is not None:
                self.pending_mm2()
                self.pending_mm2 = None
            self._F_quant2(t)
            self._F_mm2_defer(seq, t, store_out)
        self.pending_mm2()
        self.pending_mm2 = None

    def _xr(self, dc, c, w):
        return self.S["xr"][:, dc * SEQ + c: dc * SEQ + c + w]

    def emit_Q1(self, ch, vx):
        E, v, pb, S = self.E, self.v, self.pb, self.S
        q = self.q1stats
        Q1W = self.Q1W
        q0 = ch * Q1W
        xh16q, xq16q, xcq, k1i8q = (S["xh16q"], S["xq16q"], S["xcq"],
                                    S["k1i8q"])
        k1full = S["k1full"]
        ones16, epsc, mw1c = S["ones16"], S["epsc"], S["mw1c"]
        qs = lambda n: q[n][:, 0:Q1W]
        for dc in range(NDC):
            E("scalar", lambda Sc, dc=dc: Sc.activation(
                xh16q[:, dc * Q1W:(dc + 1) * Q1W], self._xr(dc, q0, Q1W),
                AF.Copy),
                waits=[vx] if (dc == 0 and vx is not None) else [])
        for dc in range(NDC):
            E("scalar", lambda Sc, dc=dc: Sc.activation(
                xq16q[:, dc * Q1W:(dc + 1) * Q1W],
                xh16q[:, dc * Q1W:(dc + 1) * Q1W], AF.Square),
                inc=("tS", 1) if dc == NDC - 1 else None)
        s_xh = ("tS", v("tS"))
        self.q1_xr_guard = s_xh
        for dc in range(NDC):
            E("tensor", lambda T, dc=dc: T.matmul(
                pb[6][:, 0:Q1W], ones16[:, :],
                xh16q[:, dc * Q1W:(dc + 1) * Q1W],
                start=(dc == 0), stop=(dc == NDC - 1)),
                waits=self.wbank(6, [s_xh]) if dc == 0 else [])
        for dc in range(NDC):
            E("tensor", lambda T, dc=dc: T.matmul(
                pb[7][:, 0:Q1W], ones16[:, :],
                xq16q[:, dc * Q1W:(dc + 1) * Q1W],
                start=(dc == 0), stop=(dc == NDC - 1)),
                waits=self.wbank(7, []) if dc == 0 else [],
                inc=("tT", 1) if dc == NDC - 1 else None)
        t_sums = ("tT", v("tT"))
        self.bank_ready[6] = t_sums
        self.bank_ready[7] = t_sums
        mw_ = self.rbank(6, [])
        if self.q1_marks:
            mw_.append(self.q1_marks[max(self.q1_marks)])
        E("scalar", lambda Sc: Sc.activation(
            qs("mean_c"), pb[6][:, 0:Q1W], AF.Copy, scale=1.0 / DM),
            waits=mw_)
        s_m = E("scalar", lambda Sc: Sc.activation(
            qs("tmp_c"), pb[7][:, 0:Q1W], AF.Copy, scale=1.0 / DM),
            waits=self.rbank(7, []), inc=("tS", 1))
        self.bank_free[6] = s_m
        self.bank_free[7] = s_m
        E("vector", lambda V: V.tensor_tensor(
            qs("tmp2_c"), qs("mean_c"), qs("mean_c"), ALU.mult), waits=[s_m])
        E("vector", lambda V: V.tensor_tensor(
            qs("tmp_c"), qs("tmp_c"), qs("tmp2_c"), ALU.subtract))
        v_var = E("vector", lambda V: V.tensor_scalar(
            qs("tmp_c"), qs("tmp_c"), 0.0, None, ALU.max), inc=("tV", 1))
        E("scalar", lambda Sc: Sc.activation(
            qs("tmp2_c"), qs("tmp_c"), AF.Ln, bias=epsc[:, 0:1]),
            waits=[v_var])
        s_r = E("scalar", lambda Sc: Sc.activation(
            qs("rstd_c"), qs("tmp2_c"), AF.Exp, scale=-0.5), inc=("tS", 1))
        for dc in range(NDC):
            E("vector", lambda V, dc=dc, q0=q0: V.tensor_tensor(
                xcq[:, dc * Q1W:(dc + 1) * Q1W], self._xr(dc, q0, Q1W),
                qs("mean_c"), ALU.subtract),
                waits=[s_r] if dc == 0 else [])
        v_m1w = E("vector", lambda V: V.tensor_reduce(
            qs("m1_c"),
            bass.AP(xcq, 0, [[NDC * Q1W, 128], [1, Q1W], [Q1W, NDC]]),
            AX.X, ALU.max, apply_absolute_value=True), inc=("tV", 1))
        self.repl_max(q["m1_c"], 0, Q1W, q["a_c"], v_m1w)
        E("vector", lambda V: V.tensor_tensor(
            qs("tmp_c"), qs("a_c"), qs("rstd_c"), ALU.mult))
        E("vector", lambda V: V.tensor_scalar(
            qs("tmp_c"), qs("tmp_c"), 1e-5, None, ALU.max))
        vq_r = E("vector", lambda V: V.reciprocal(
            qs("tmp2_c"), qs("tmp_c")), inc=("tV", 1))
        E("vector", lambda V: V.scalar_tensor_tensor(
            qs("q_c"), qs("tmp2_c"), 127.0, qs("rstd_c"),
            ALU.mult, ALU.mult), waits=[vq_r])
        E("vector", lambda V: V.tensor_scalar(
            qs("g1_c"), qs("tmp_c"), mw1c[:, 0:1], None, ALU.mult))
        for dc in range(NDC):
            E("vector", lambda V, dc=dc: V.tensor_tensor(
                k1i8q[:, dc * Q1W:(dc + 1) * Q1W],
                xcq[:, dc * Q1W:(dc + 1) * Q1W], qs("q_c"), ALU.mult))
        for dc in range(NDC):
            E("vector", lambda V, dc=dc, q0=q0: V.tensor_tensor(
                k1full[:, dc * SEQ + q0:dc * SEQ + q0 + Q1W],
                k1i8q[:, dc * Q1W:(dc + 1) * Q1W], qs("g1_c"), ALU.mult),
                inc=("tV", 1) if dc == NDC - 1 else None)
        self.q1_marks[ch] = ("tV", v("tV"))
        if DBGS and DBGS_Q1:
            dmark = None
            for j, nm in enumerate(("q_c", "g1_c", "a_c", "rstd_c", "mean_c")):
                dmark = self.dma("scalar", "dC",
                                 self.dbg_q1[self.fidx, ch, j, :],
                                 q[nm][0:1, 0:Q1W],
                                 waits=[self.q1_marks[ch]] if j == 0 else [])
            self.q1_dump = dmark
            E("vector", lambda V: V.engine_nop(), waits=[dmark])

    def _F_mm1(self, seq, t):
        E, v, dma, pb, S = self.E, self.v, self.dma, self.pb, self.S
        a0 = t * FT
        lo = max(a0 - 1, 0)
        hi = min(a0 + FT + 1, SEQ)
        W = hi - lo
        c0 = lo - (a0 - 1)
        hs, cbuf, hp = S["hs"], S["cbuf"], S["hp"]
        k1full, ring1 = S["k1full"], S["ring1"]
        cw, cb = S["cw"], S["cb"]
        need_ch = (a0 + FT) // self.Q1W
        v_k1 = self.q1_marks[min(need_ch, self.NQ1C - 1)]
        if t == 0:
            for sl in range(2):
                E("vector", lambda V, sl=sl: V.memset(
                    hs[:, sl * FW:sl * FW + 1], 0.0))
            self.vflush()
        if t == NFT - 1:
            for sl in range(2):
                E("vector", lambda V, sl=sl: V.memset(
                    hs[:, sl * FW + FW - 1:sl * FW + FW], 0.0))
            self.vflush()
        conv_done = {}
        silu_done = {}
        evac_done = {}
        pend_silu = None
        for cc in range(NCC):
            g = cc // 4
            slot = g % 2
            sl = cc % 2
            if cc % 4 == 0:
                self._ring_load(1, g, slot)
            bank = 4 + sl
            waits = []
            if cc == 0:
                waits.append(v_k1)
            if cc % 4 == 0:
                waits.append(self.ring_marks[("r1", slot)])
            if cc >= 2:
                waits.append(evac_done[cc - 2])
            elif self.evac_prev is not None:
                waits.append(self.evac_prev[cc])
            for dc in range(NDC):
                E("tensor", lambda T, dc=dc, cc=cc, slot=slot, bank=bank,
                  lo=lo, W=W, c0=c0: T.matmul(
                      pb[bank][:, c0:c0 + W],
                      ring1[:, slot * 4096 + dc * 512 + (cc % 4) * 128:
                            slot * 4096 + dc * 512 + (cc % 4) * 128 + 128],
                      k1full[:, dc * SEQ + lo:dc * SEQ + lo + W],
                      start=(dc == 0), stop=(dc == NDC - 1)),
                  waits=waits if dc == 0 else [],
                  inc=("tT", 1) if dc == NDC - 1 else None)
            t_mm1 = ("tT", v("tT"))
            if cc % 4 == 3:
                self.ring_prev[("r1", slot)] = t_mm1
            hw_ = [t_mm1]
            if cc >= 2:
                hw_.append(conv_done[cc - 2])
            elif self.conv_prev is not None:
                hw_.append(self.conv_prev[cc])
            evac_done[cc] = E("scalar", lambda Sc, sl=sl, bank=bank, c0=c0,
                              W=W: Sc.activation(
                hs[:, sl * FW + c0:sl * FW + c0 + W],
                pb[bank][:, c0:c0 + W], AF.Copy), waits=hw_, inc=("tS", 1))
            cwait = [evac_done[cc]]
            if cc >= 2:
                cwait.append(silu_done[cc - 2])
            elif self.cbuf_prev is not None:
                cwait.append(self.cbuf_prev[cc])
            cbo = sl * FT
            E("vector", lambda V, cc=cc, cbo=cbo, sl=sl: V.tensor_scalar(
                cbuf[:, cbo:cbo + FT], hs[:, sl * FW:sl * FW + FT],
                cw[:, cc * 3:cc * 3 + 1], cb[:, cc:cc + 1],
                ALU.mult, ALU.add), waits=cwait)
            E("vector", lambda V, cc=cc, cbo=cbo, sl=sl: V.scalar_tensor_tensor(
                cbuf[:, cbo:cbo + FT], hs[:, sl * FW + 1:sl * FW + FT + 1],
                cw[:, cc * 3 + 1:cc * 3 + 2], cbuf[:, cbo:cbo + FT],
                ALU.mult, ALU.add))
            vc = E("vector", lambda V, cc=cc, cbo=cbo, sl=sl:
                   V.scalar_tensor_tensor(
                cbuf[:, cbo:cbo + FT], hs[:, sl * FW + 2:sl * FW + FT + 2],
                cw[:, cc * 3 + 2:cc * 3 + 3], cbuf[:, cbo:cbo + FT],
                ALU.mult, ALU.add), inc=("tV", 1))
            conv_done[cc] = vc
            if pend_silu is not None:
                pcc, pvc = pend_silu
                sw_ = [pvc]
                if pcc in self.k2_prev:
                    sw_.append(self.k2_prev[pcc])
                silu_done[pcc] = E("scalar", lambda Sc, pcc=pcc: Sc.activation(
                    hp[:, pcc * FT:pcc * FT + FT],
                    cbuf[:, (pcc % 2) * FT:(pcc % 2) * FT + FT], AF.Silu),
                    waits=sw_, inc=("tS", 1))
            pend_silu = (cc, vc)
        pcc, pvc = pend_silu
        sw_ = [pvc]
        if pcc in self.k2_prev:
            sw_.append(self.k2_prev[pcc])
        silu_done[pcc] = E("scalar", lambda Sc, pcc=pcc: Sc.activation(
            hp[:, pcc * FT:pcc * FT + FT],
            cbuf[:, (pcc % 2) * FT:(pcc % 2) * FT + FT], AF.Silu),
            waits=sw_, inc=("tS", 1))
        self.s_silu = ("tS", v("tS"))
        self.cbuf_prev = (silu_done[NCC - 2], silu_done[NCC - 1])
        self.conv_prev = (conv_done[NCC - 2], conv_done[NCC - 1])
        self.evac_prev = (evac_done[NCC - 2], evac_done[NCC - 1])
        self.silu_done_t = silu_done

    def _F_quant2(self, t):
        E, v, pb, S = self.E, self.v, self.pb, self.S
        st = self.stats
        hp, k2i8, k2bf = S["hp"], S["k2i8"], S["k2bf"]
        mw2ch = S["mw2ch"]
        v_m2 = E("vector", lambda V: V.tensor_reduce(
            st["m2_b"][:, 0:FT],
            bass.AP(hp, 0, [[NCC * FT, 128], [1, FT], [FT, NCC]]),
            AX.X, ALU.max, apply_absolute_value=True),
            waits=[self.s_silu], inc=("tV", 1))
        self.repl_max(st["m2_b"], 0, FT, st["a2_b"], v_m2)
        E("vector", lambda V: V.tensor_scalar(
            st["a2_b"][:, 0:FT], st["a2_b"][:, 0:FT], 1e-5, None, ALU.max))
        vq2_r = E("vector", lambda V: V.reciprocal(
            st["m2_b"][:, 0:FT], st["a2_b"][:, 0:FT]), inc=("tV", 1))
        E("vector", lambda V: V.tensor_scalar(
            st["s2_b"][:, 0:FT], st["m2_b"][:, 0:FT], 127.0, None, ALU.mult),
            waits=[vq2_r])
        E("vector", lambda V: V.tensor_scalar(
            st["g2_b"][:, 0:FT], st["a2_b"][:, 0:FT], mw2ch[:, 0:1], None,
            ALU.mult), inc=("tV", 1))
        v_q2 = ("tV", v("tV"))
        if DBGS:
            dmark = None
            for j, nm in enumerate(("s2_b", "g2_b", "a2_b")):
                dmark = self.dma("scalar", "dC",
                                 self.dbg_q2[self.fidx, t, j, :],
                                 st[nm][0:1, 0:FT],
                                 waits=[v_q2] if j == 0 else [])
            E("vector", lambda V: V.engine_nop(), waits=[dmark])
        self.k2v = {}
        k2_cur = {}
        for cc in range(NCC):
            E("vector", lambda V, cc=cc: V.tensor_tensor(
                k2i8[:, cc * FT:cc * FT + FT], hp[:, cc * FT:cc * FT + FT],
                st["s2_b"][:, 0:FT], ALU.mult),
                waits=([v_q2] + ([self.mm2_mark] if self.mm2_mark else []))
                if cc == 0 else [], inc=("tV", 1))
            k2_cur[cc] = ("tV", v("tV"))
            E("scalar", lambda Sc, cc=cc: Sc.activation(
                k2bf[:, cc * FT:cc * FT + FT], k2i8[:, cc * FT:cc * FT + FT],
                AF.Copy), waits=[k2_cur[cc]], inc=("tS", 1))
            self.k2v[cc] = ("tS", v("tS"))
        self.k2_prev = k2_cur

    def _F_mm2_defer(self, seq, t, store_out):
        E, v, dma, pb, S = self.E, self.v, self.dma, self.pb, self.S
        st = self.stats
        k2bf, ring2 = S["k2bf"], S["ring2"]
        a0 = t * FT
        k2v = dict(self.k2v)
        xr_guard = self.q1_xr_guard

        def doit():
            for cc in range(NCC):
                g = cc // 4
                slot = g % 2
                if cc % 4 == 0:
                    self._ring_load(2, g, slot)
                waits = [k2v[cc]]
                if cc % 4 == 0:
                    waits.append(self.ring_marks[("r2", slot)])
                if cc == 0:
                    for bk in range(4):
                        waits += self.wbank(bk, [])
                for dm in range(NDC):
                    bank = dm // 2
                    half = (dm % 2) * FT
                    E("tensor", lambda T, cc=cc, dm=dm, bank=bank, half=half,
                      slot=slot: T.matmul(
                          pb[bank][:, half:half + FT],
                          ring2[:, slot * 4096 + (cc % 4) * 1024 + dm * 128:
                                slot * 4096 + (cc % 4) * 1024 + dm * 128 + 128],
                          k2bf[:, cc * FT:cc * FT + FT],
                          start=(cc == 0 and dm % 2 == 0),
                          stop=(cc == NCC - 1 and dm % 2 == 1)),
                      waits=waits if dm == 0 else [],
                      inc=("tT", 1) if (dm == NDC - 1 and cc % 4 == 3) else None)
                if cc % 4 == 3:
                    self.ring_prev[("r2", slot)] = ("tT", v("tT"))
            t_mm2 = ("tT", v("tT"))
            self.mm2_mark = t_mm2
            for bk in range(4):
                self.bank_ready[bk] = t_mm2
            for dm in range(NDC):
                bank = dm // 2
                half = (dm % 2) * FT
                ew = self.rbank(bank, []) if dm % 2 == 0 else []
                if dm == 0 and xr_guard is not None:
                    ew = ew + [xr_guard]
                E("vector", lambda V, bank=bank, half=half: V.tensor_tensor(
                    st["a2_b"][:, 0:FT], pb[bank][:, half:half + FT],
                    st["g2_b"][:, 0:FT], ALU.mult), waits=ew)
                E("vector", lambda V, dm=dm: V.tensor_tensor(
                    self._xr(dm, a0, FT), self._xr(dm, a0, FT),
                    st["a2_b"][:, 0:FT], ALU.add),
                    inc=("tV", 1) if dm == NDC - 1 else None)
            v_tile = ("tV", v("tV"))
            for bk in range(4):
                self.bank_free[bk] = v_tile
            if store_out:
                dstr = bass.AP(self.ofm, seq * DM * SEQ + a0,
                               [[SEQ, 128], [128 * SEQ, NDC], [1, FT]])
                srcr = bass.AP(S["xr"], a0,
                               [[NDC * SEQ, 128], [SEQ, NDC], [1, FT]])
                dma("scalar", "dC", dstr, srcr, waits=[v_tile])
        self.pending_mm2 = doit

    def _ring_load(self, which, g, slot):
        S, dma = self.S, self.dma
        key = ("r%d" % which, slot)
        waits = [self.scr_marks[(which, g % 8)]]
        if key in self.ring_prev:
            waits.append(self.ring_prev[key])
        eng, q = (("sync", "dS") if which == 1 else ("gpsimd", "dG"))
        if which == 1:
            src = bass.AP(self.t1s, (g % 8) * 128 * 4096,
                          [[4096, 128], [1, 4096]])
            dst = S["ring1"][:, slot * 4096:slot * 4096 + 4096]
        else:
            src = bass.AP(self.t2s, (g % 8) * 128 * 4096,
                          [[4096, 128], [1, 4096]])
            dst = S["ring2"][:, slot * 4096:slot * 4096 + 4096]
        self.ring_marks[key] = dma(eng, q, dst, src, waits=waits)

    def emit_A(self, seq):
        E, v, dma, pb, S = self.E, self.v, self.dma, self.pb, self.S
        ast = self.astats
        AST = self.AST
        at = lambda n: ast[n][:, 0:AST]
        qt_, kt_, vge, vgo = S["qt_"], S["kt_"], S["vge"], S["vgo"]
        osb, wqblk, vstage = S["osb"], S["wqblk"], S["vstage"]
        xnfull, ah16, aq16 = S["xnfull"], S["ah16"], S["aq16"]
        wo_sb = S["wo_sb"]
        ones16, onesm, onesbf, ident = (S["ones16"], S["onesm"], S["onesbf"],
                                        S["ident"])
        epsc, cneg, bq, bo = S["epsc"], S["cneg"], S["bq"], S["bo"]
        pte, pto, zrow, zhl, zbsb = (S["pte"], S["pto"], S["zrow"], S["zhl"],
                                     S["zbsb"])
        pte2, pto2 = S["pte2"], S["pto2"]
        pteo, pteo2 = S["pteo"], S["pteo2"]
        bw = self.barrier_waits()
        self.vstage_free = None
        self.pv_mark = {}
        self.pending_bcast = None

        for kc in range(NKC):
            for p in range(4):
                eo = (kc * 4 + p) * 65
                oo = (kc * 4 + p) * 128
                E("gpsimd", lambda g, eo=eo: g.memset(
                    vge[:, eo + 64:eo + 65], 1.0),
                    waits=bw if (kc == 0 and p == 0) else [])
                E("gpsimd", lambda g, oo=oo: g.memset(vgo[:, oo:oo + 1], 1.0))
                E("gpsimd", lambda g, oo=oo: g.memset(
                    vgo[:, oo + 1:oo + 64], 0.0))
        E("gpsimd", lambda g: g.memset(zrow[0:1, 0:1], 0.0), inc=("tG", 1))
        g_aug = ("tG", v("tG"))

        for t8 in range(SEQ // AST):
            t0 = t8 * AST
            for dc in range(NDC):
                E("vector", lambda V, dc=dc, t0=t0: V.tensor_copy(
                    ah16[:, dc * AST:dc * AST + AST], self._xr(dc, t0, AST)),
                    waits=bw if (t8 == 0 and dc == 0) else [])
            for dc in range(NDC):
                E("vector", lambda V, dc=dc: V.tensor_tensor(
                    aq16[:, dc * AST:dc * AST + AST],
                    ah16[:, dc * AST:dc * AST + AST],
                    ah16[:, dc * AST:dc * AST + AST], ALU.mult),
                    inc=("tV", 1) if dc == NDC - 1 else None)
            v16 = ("tV", v("tV"))
            for dc in range(NDC):
                E("tensor", lambda T, dc=dc: T.matmul(
                    pb[6][:, 0:AST], ones16[:, :], ah16[:, dc * AST:dc * AST + AST],
                    start=(dc == 0), stop=(dc == NDC - 1)),
                    waits=self.wbank(6, [v16]) if dc == 0 else [])
            for dc in range(NDC):
                E("tensor", lambda T, dc=dc: T.matmul(
                    pb[7][:, 0:AST], ones16[:, :], aq16[:, dc * AST:dc * AST + AST],
                    start=(dc == 0), stop=(dc == NDC - 1)),
                    waits=self.wbank(7, []) if dc == 0 else [],
                    inc=("tT", 1) if dc == NDC - 1 else None)
            tsum = ("tT", v("tT"))
            self.bank_ready[6] = tsum
            self.bank_ready[7] = tsum
            E("scalar", lambda Sc: Sc.activation(
                at("amean"), pb[6][:, 0:AST], AF.Copy, scale=1.0 / DM),
                waits=self.rbank(6, []))
            sm = E("scalar", lambda Sc: Sc.activation(
                at("atmp"), pb[7][:, 0:AST], AF.Copy, scale=1.0 / DM),
                waits=self.rbank(7, []), inc=("tS", 1))
            self.bank_free[6] = sm
            self.bank_free[7] = sm
            E("vector", lambda V: V.tensor_tensor(
                at("atmp2"), at("amean"), at("amean"), ALU.mult), waits=[sm])
            E("vector", lambda V: V.tensor_tensor(
                at("atmp"), at("atmp"), at("atmp2"), ALU.subtract))
            vv = E("vector", lambda V: V.tensor_scalar(
                at("atmp"), at("atmp"), 0.0, None, ALU.max), inc=("tV", 1))
            E("scalar", lambda Sc: Sc.activation(
                at("atmp2"), at("atmp"), AF.Ln, bias=epsc[:, 0:1]), waits=[vv])
            sr = E("scalar", lambda Sc: Sc.activation(
                at("arstd"), at("atmp2"), AF.Exp, scale=-0.5), inc=("tS", 1))
            for dc in range(NDC):
                E("vector", lambda V, dc=dc, t0=t0: V.tensor_tensor(
                    at("atmp2"), self._xr(dc, t0, AST), at("amean"),
                    ALU.subtract),
                    waits=[sr] if dc == 0 else [])
                E("vector", lambda V, dc=dc, t0=t0: V.tensor_tensor(
                    xnfull[:, dc * SEQ + t0:dc * SEQ + t0 + AST], at("atmp2"),
                    at("arstd"), ALU.mult),
                    inc=("tV", 1) if dc == NDC - 1 else None)
        v_xn = ("tV", v("tV"))
        E("gpsimd", lambda g: g.memset(zbsb[:, :], 1.0),
          waits=[v_xn], inc=("tG", 1))
        self.zbsb_claim = ("tG", v("tG"))

        wq_used = {0: None, 1: None}
        exp_done = {0: None, 1: None}
        for hg in range(2):
            fcs = ([hg * 4 + j for j in range(4)] +
                   [8 + hg * 4 + j for j in range(4)] +
                   [16 + hg * 4 + j for j in range(4)])
            for i, fc in enumerate(fcs):
                slot = i % 2
                srcw = bass.AP(self.wqkv_r, fc * DM * 128,
                               [[128, 128], [128 * 128, NDC], [1, 128]])
                lw = [v_xn] if (hg == 0 and i == 0) else []
                if wq_used[slot] is not None:
                    lw.append(wq_used[slot])
                vld = dma("gpsimd", "dG",
                          wqblk[:, slot * 1024:slot * 1024 + 1024], srcw,
                          waits=lw)
                role = "q" if fc < 8 else ("k" if fc < 16 else "v")
                lc = fc - (0 if fc < 8 else (8 if fc < 16 else 16)) - hg * 4
                for tt in range(NQT):
                    t0 = tt * QT
                    bank = 4 + ((i * NQT + tt) % 2)
                    waits = [vld, v_xn] if tt == 0 else []
                    waits = self.wbank(bank, waits)
                    for dc in range(NDC):
                        E("tensor", lambda T, dc=dc, slot=slot, bank=bank,
                          t0=t0: T.matmul(
                              pb[bank][:, 0:QT],
                              wqblk[:, slot * 1024 + dc * 128:
                                    slot * 1024 + dc * 128 + 128],
                              xnfull[:, dc * SEQ + t0:dc * SEQ + t0 + QT],
                              start=(dc == 0), stop=(dc == NDC - 1)),
                            waits=waits if dc == 0 else [],
                            inc=("tT", 1) if dc == NDC - 1 else None)
                    tq = ("tT", v("tT"))
                    self.bank_ready[bank] = tq
                    if tt == NQT - 1:
                        wq_used[slot] = tq
                    if role == "q":
                        dstp = qt_[:, lc * SEQ + t0:lc * SEQ + t0 + QT]
                    elif role == "k":
                        dstp = kt_[:, lc * SEQ + t0:lc * SEQ + t0 + QT]
                    else:
                        dstp = vstage[:, 0:QT]
                    evw = self.rbank(bank, [])
                    if role == "v" and self.vstage_free is not None:
                        evw.append(self.vstage_free)
                    sev = E("vector", lambda V, fc=fc, dstp=dstp, bank=bank:
                            V.tensor_scalar(dstp, pb[bank][:, 0:QT],
                                            bq[:, fc:fc + 1], None, ALU.add),
                            waits=evw, inc=("tV", 1))
                    self.bank_free[bank] = sev
                    if role == "v":
                        for tb in range(QT // 128):
                            kc = (t0 + tb * 128) // 128
                            tt2 = E("tensor", lambda T, tb=tb: T.transpose(
                                pb[7][:, 0:128],
                                vstage[:, tb * 128:tb * 128 + 128],
                                ident[:, :]),
                                waits=self.wbank(7, [sev, g_aug]
                                                 if tb == 0 else []),
                                inc=("tT", 1))
                            self.bank_ready[7] = tt2
                            E("vector", lambda V, kc=kc, lc=lc: V.tensor_copy(
                                vge[:, (kc * 4 + lc) * 65:(kc * 4 + lc) * 65 + 64],
                                pb[7][:, 0:64]), waits=self.rbank(7, []))
                            vt2 = E("vector", lambda V, kc=kc, lc=lc:
                                    V.tensor_copy(
                                        vgo[:, (kc * 4 + lc) * 128 + 64:
                                            (kc * 4 + lc) * 128 + 128],
                                        pb[7][:, 64:128]), inc=("tV", 1))
                            self.bank_free[7] = vt2
                            if tb == QT // 128 - 1:
                                self.vstage_free = tt2
            v_sweep = ("tV", v("tV"))
            s_sweep = ("tS", v("tS"))


            def emit_pv(prev, p):
                se, so, kc = prev
                par = kc % 2
                wpe = self.wbank(2, [se]) if kc == 0 else [se]
                E("tensor", lambda T, p=p, kc=kc, par=par: T.matmul(
                    pb[2][0:65, 0:QT],
                    vge[:, (kc * 4 + p) * 65:(kc * 4 + p) * 65 + 65],
                    (pteo if par == 0 else pteo2)[:, 0:QT],
                    start=(kc == 0), stop=(kc == NKC - 1)),
                    waits=wpe)
                wpo = self.wbank(3, [so]) if kc == 0 else [so]
                E("tensor", lambda T, p=p, kc=kc, par=par: T.matmul(
                    pb[3][0:128, 0:QT],
                    vgo[:, (kc * 4 + p) * 128:(kc * 4 + p) * 128 + 128],
                    (pteo if par == 0 else pteo2)[:, QT:2 * QT],
                    start=(kc == 0), stop=(kc == NKC - 1)),
                    waits=wpo, inc=("tT", 1))
                self.pv_mark[par] = ("tT", v("tT"))
            for qt in range(NQT):
                q0 = qt * QT
                for p in range(4):
                    gp = hg * 4 + p
                    prev = None
                    for kc in range(NKC):
                        par = kc % 2
                        waits = self.wbank(0, [exp_done[0]]
                                           if exp_done[0] else [])
                        if qt == 0 and p == 0 and kc == 0:
                            waits += [v_sweep, s_sweep]
                        E("tensor", lambda T, p=p, kc=kc, q0=q0: T.matmul(
                            pb[0][:, 0:QT],
                            kt_[0:64, p * SEQ + kc * 128:p * SEQ + kc * 128 + 128],
                            qt_[0:64, p * SEQ + q0:p * SEQ + q0 + QT],
                            start=True, stop=True),
                            waits=waits, inc=("tT", 1))
                        E("tensor", lambda T, p=p, kc=kc, q0=q0: T.matmul(
                            pb[1][:, 0:QT],
                            kt_[64:128, p * SEQ + kc * 128:
                                p * SEQ + kc * 128 + 128],
                            qt_[64:128, p * SEQ + q0:p * SEQ + q0 + QT],
                            start=True, stop=True),
                            inc=("tT", 1))
                        tso = ("tT", v("tT"))
                        ow = [tso]
                        if self.pv_mark.get(par) is not None:
                            ow.append(self.pv_mark[par])
                        so = E("scalar", lambda Sc, par=par: Sc.activation(
                            (pteo if par == 0 else pteo2)[:, :],
                            self.pb01[:, 0:2 * QT], AF.Exp, scale=0.125,
                            bias=cneg[:, 0:1]), waits=ow, inc=("tS", 1))
                        exp_done[0] = so
                        exp_done[1] = so
                        self.bank_free[0] = so
                        self.bank_free[1] = so
                        if prev is not None:
                            emit_pv(prev, p)
                        if kc == 6 and self.pending_bcast is not None:
                            self.pending_bcast()
                            self.pending_bcast = None
                        prev = (so, so, kc)
                    emit_pv(prev, p)
                    t_pv = self.pv_mark[(NKC - 1) % 2]
                    self.bank_ready[2] = t_pv
                    self.bank_ready[3] = t_pv
                    stw = self.rbank(2, self.rbank(3, []))
                    stw += self.wbank(5, self.wbank(6, []))
                    E("vector", lambda V: V.tensor_copy(
                        pb[5][0:65, 0:QT], pb[2][0:65, 0:QT]), waits=stw)
                    vst = E("vector", lambda V: V.tensor_copy(
                        pb[6][:, 0:QT], pb[3][:, 0:QT]), inc=("tV", 1))
                    self.bank_free[2] = vst
                    self.bank_free[3] = vst
                    zcw = []
                    if self.zbsb_claim is not None:
                        zcw.append(self.zbsb_claim)
                        self.zbsb_claim = None
                    E("vector", lambda V: V.tensor_copy(
                        zbsb[64:65, 0:QT], pb[5][64:65, 0:QT]), waits=zcw)
                    E("vector", lambda V: V.tensor_copy(
                        zbsb[0:1, 0:QT], pb[6][0:1, 0:QT]))
                    E("vector", lambda V: V.reciprocal(
                        zrow[0:65, 0:QT], zbsb[0:65, 0:QT]))
                    E("vector", lambda V: V.tensor_copy(
                        zhl[64:65, 0:QT], zrow[64:65, 0:QT]))
                    E("vector", lambda V: V.tensor_tensor(
                        zhl[64:65, QT:2 * QT], zrow[64:65, 0:QT],
                        zhl[64:65, 0:QT], ALU.subtract))
                    E("vector", lambda V: V.tensor_copy(
                        zhl[0:1, 0:QT], zrow[0:1, 0:QT]))
                    vz = E("vector", lambda V: V.tensor_tensor(
                        zhl[0:1, QT:2 * QT], zrow[0:1, 0:QT],
                        zhl[0:1, 0:QT], ALU.subtract), inc=("tV", 1))

                    ob = (qt * 8 + gp) * QT

                    def tail_bcast(vz=vz, ob=ob):
                        E("tensor", lambda T: T.matmul(
                            pb[4][0:64, 0:QT], onesbf[64:65, 0:64],
                            zhl[64:65, 0:QT], start=True, stop=False),
                            waits=self.wbank(4, [vz]))
                        E("tensor", lambda T: T.matmul(
                            pb[4][0:64, 0:QT], onesbf[64:65, 0:64],
                            zhl[64:65, QT:2 * QT], start=False, stop=True))
                        E("tensor", lambda T: T.matmul(
                            pb[4][64:128, 0:QT], onesbf[0:1, 0:64],
                            zhl[0:1, 0:QT], start=True, stop=False))
                        tz = E("tensor", lambda T: T.matmul(
                            pb[4][64:128, 0:QT], onesbf[0:1, 0:64],
                            zhl[0:1, QT:2 * QT], start=False, stop=True),
                            inc=("tT", 1))
                        self.bank_ready[4] = tz
                        vzb = E("vector", lambda V: V.tensor_copy(
                            zbsb[:, 0:QT], pb[4][:, 0:QT]),
                            waits=self.rbank(4, []), inc=("tV", 1))
                        self.bank_free[4] = vzb
                        E("vector", lambda V: V.tensor_tensor(
                            osb[0:64, ob:ob + QT], pb[5][0:64, 0:QT],
                            zbsb[0:64, 0:QT], ALU.mult))
                        vo = E("vector", lambda V: V.tensor_tensor(
                            osb[64:128, ob:ob + QT], pb[6][64:128, 0:QT],
                            zbsb[64:128, 0:QT], ALU.mult), inc=("tV", 1))
                        self.bank_free[5] = vo
                        self.bank_free[6] = vo
                    self.pending_bcast = tail_bcast
            if self.pending_bcast is not None:
                self.pending_bcast()
                self.pending_bcast = None
        v_core = ("tV", v("tV"))

        srcw = bass.AP(self.wout, 0, [[DM, 128], [128 * DM, NDC], [1, DM]])
        vwo = dma("gpsimd", "dG", wo_sb[:, :],
                  srcw, waits=[("tT", v("tT")), v_core])
        for qt in range(NQT):
            q0 = qt * QT
            for dm in range(NDC):
                bank = 2 + (dm % 2)
                waits = [v_core, vwo] if (qt == 0 and dm == 0) else []
                waits = self.wbank(bank, waits)
                for dc in range(NDC):
                    E("tensor", lambda T, dm=dm, dc=dc, bank=bank, qt=qt:
                      T.matmul(
                          pb[bank][:, 0:QT],
                          wo_sb[:, dc * DM + dm * 128:dc * DM + dm * 128 + 128],
                          osb[:, (qt * 8 + dc) * QT:(qt * 8 + dc) * QT + QT],
                          start=(dc == 0), stop=(dc == NDC - 1)),
                        waits=waits if dc == 0 else [],
                        inc=("tT", 1) if dc == NDC - 1 else None)
                to = ("tT", v("tT"))
                self.bank_ready[bank] = to
                vo2 = E("vector", lambda V, dm=dm, bank=bank, q0=q0:
                        V.scalar_tensor_tensor(
                            self._xr(dm, q0, QT), pb[bank][:, 0:QT],
                            bo[:, dm:dm + 1], self._xr(dm, q0, QT),
                            ALU.add, ALU.add),
                        waits=self.rbank(bank, []),
                        inc=("tV", 1))
                self.bank_free[bank] = vo2


    def finalize(self):
        nc = self.nc
        sems = {}
        import contextlib
        stack = contextlib.ExitStack()
        for sname in ("dS", "dC", "dG", "tT", "tV", "tS", "tG"):
            sems[sname] = stack.enter_context(nc.semaphore(sname))
        block = stack.enter_context(nc.Block())

        def runner(eng_name):
            def run(eng):
                for waits, build, inc in self.ops[eng_name]:
                    for (sn, val) in waits:
                        eng.wait_ge(sems[sn], val)
                    instr = build(eng)
                    if inc is not None:
                        assert instr is not None, (eng_name, inc)
                        instr.then_inc(sems[inc[0]], inc[1])
            return run

        block.sync(runner("sync"))
        block.tensor(runner("tensor"))
        block.vector(runner("vector"))
        block.scalar(runner("scalar"))
        block.gpsimd(runner("gpsimd"))
        stack.close()
        return nc


def build_nc():
    b = Builder()
    return b.finalize()


_NC_CACHE = {}


def kernel(L0_attn_x, L0_attn_w_qkv, L0_attn_b_qkv, L0_attn_w_out, L0_attn_b_out,
           L0_ffn_w1, L0_ffn_conv_w, L0_ffn_conv_b, L0_ffn_w2):
    x = np.asarray(L0_attn_x, dtype=np.float32)
    wqkv = np.asarray(L0_attn_w_qkv, dtype=np.float32)
    wqkv_r = np.ascontiguousarray(wqkv.reshape(DM, 24, 128).transpose(1, 0, 2))
    bqkv = np.asarray(L0_attn_b_qkv, dtype=np.float32)
    wout = np.ascontiguousarray(np.asarray(L0_attn_w_out, dtype=np.float32))
    bout = np.asarray(L0_attn_b_out, dtype=np.float32)
    w1 = np.asarray(L0_ffn_w1, dtype=np.float32)
    convw = np.asarray(L0_ffn_conv_w, dtype=np.float32).reshape(DH, 3)
    convb = np.asarray(L0_ffn_conv_b, dtype=np.float32)
    w2 = np.asarray(L0_ffn_w2, dtype=np.float32)

    w1t = np.ascontiguousarray(w1.T)
    w2t = np.ascontiguousarray(w2.T)
    bqr = np.ascontiguousarray(bqkv.reshape(24, 128).T)
    bor = np.ascontiguousarray(bout.reshape(8, 128).T)
    cwr = np.ascontiguousarray(
        convw.reshape(32, 128, 3).transpose(1, 0, 2).reshape(128, 96))
    cbr = np.ascontiguousarray(convb.reshape(32, 128).T)

    if "nc" not in _NC_CACHE:
        _NC_CACHE["nc"] = build_nc()
    nc = _NC_CACHE["nc"]

    in_maps = []
    for c in range(8):
        xfm = np.ascontiguousarray(x[2 * c:2 * c + 2].transpose(0, 2, 1))
        in_maps.append({
            "xfm": xfm, "wqkv_r": wqkv_r, "wout": wout, "w1t": w1t, "w2t": w2t,
            "bqr": bqr, "bor": bor, "cwr": cwr, "cbr": cbr,
        })
    res = run_bass_kernel_spmd(nc, in_maps, core_ids=list(range(8)))
    out = np.empty((16, SEQ, DM), dtype=np.float32)
    for c in range(8):
        out[2 * c:2 * c + 2] = res.results[c]["ofm"].transpose(0, 2, 1)
    return out


# version 16 (final)
# speedup vs baseline: 1.2002x; 1.0464x over previous
import numpy as np
import ml_dtypes
import concourse.bass as bass
import concourse.mybir as mybir
from concourse.bass_utils import run_bass_kernel_spmd

F32 = mybir.dt.float32
F16 = mybir.dt.float16
BF16 = mybir.dt.bfloat16
I8 = mybir.dt.int8
F8 = mybir.dt.float8e4
AF = mybir.ActivationFunctionType
ALU = mybir.AluOpType
AX = mybir.AxisListType

SEQ = 2048
DM = 1024
DH = 4096
NDC = 8
NCC = 32
FT = 256
FW = FT + 2
NFT = SEQ // FT
QT = 512
NQT = SEQ // QT
NKC = SEQ // 128
CSHIFT = -24.0
PB = 512

ENGS = ("sync", "tensor", "vector", "scalar", "gpsimd")
PHASES = "full"
DEBUG = False
DBGS = False
DBGS_Q1 = True


class Builder:
    def __init__(self):
        self.nc = bass.Bass(target_bir_lowering=False,
                            detect_race_conditions=False)
        self.ops = {e: [] for e in ENGS}
        self.cnt = {}
        self.bank_ready = {}
        self.bank_free = {}
        self._build()

    def v(self, s):
        return self.cnt.get(s, 0)

    def E(self, eng, build, waits=(), inc=None):
        waits = [w for w in waits if w is not None]
        if inc is not None:
            self.cnt[inc[0]] = self.cnt.get(inc[0], 0) + inc[1]
        self.ops[eng].append((tuple(waits), build, inc))
        return None if inc is None else (inc[0], self.cnt[inc[0]])

    def dma(self, eng, sem, dst, src, waits=(), slow=False):
        def b(e):
            if slow:
                return e.dma_start(dst, src, allow_slow_non_contiguous=True)
            return e.dma_start(dst, src)
        self.E(eng, b, waits, (sem, 16))
        return (sem, self.v(sem))

    def wbank(self, i, waits):
        w = list(waits)
        if i in self.bank_free:
            w.append(self.bank_free[i])
        return w

    def rbank(self, i, waits):
        w = list(waits)
        if i in self.bank_ready:
            w.append(self.bank_ready[i])
        return w

    def vflush(self, waits=(), inc=None):
        S = self.S
        return self.E("vector", lambda V: V.tensor_copy(
            S["junkf"][:, 0:128], S["ident"][:, :]), waits, inc)

    def barrier_waits(self):
        return [(s, self.v(s)) for s in
                ("tT", "tV", "tS", "tG", "dS", "dC", "dG") if self.v(s) > 0]

    def phase_barrier(self):
        bw = self.barrier_waits()
        for e in ENGS:
            self.E(e, lambda eng: eng.nop(), bw)

    def _build(self):
        nc = self.nc
        E, dma, v = self.E, self.dma, self.v

        self.xfm = nc.dram_tensor("xfm", [2, DM, SEQ], F32, kind="ExternalInput")
        self.wqkv_r = nc.dram_tensor("wqkv_r", [24, DM, 128], F32,
                                     kind="ExternalInput")
        self.wout = nc.dram_tensor("wout", [DM, DM], F32, kind="ExternalInput")
        self.w1t = nc.dram_tensor("w1t", [DM, DH], F32, kind="ExternalInput")
        self.w2t = nc.dram_tensor("w2t", [DH, DM], F32, kind="ExternalInput")
        self.bqr = nc.dram_tensor("bqr", [128, 24], F32, kind="ExternalInput")
        self.bor = nc.dram_tensor("bor", [128, 8], F32, kind="ExternalInput")
        self.cwr = nc.dram_tensor("cwr", [128, 96], F32, kind="ExternalInput")
        self.cbr = nc.dram_tensor("cbr", [128, 32], F32, kind="ExternalInput")
        self.ofm = nc.dram_tensor("ofm", [2, DM, SEQ], F32, kind="ExternalOutput")
        self.t1s = nc.dram_tensor("t1s", [8, 128, 4096], F8, kind="Internal")
        if DBGS:
            self.dbg_q1 = nc.dram_tensor("dbg_q1", [4, 4, 5, 512], F32,
                                         kind="ExternalOutput")
            self.dbg_q2 = nc.dram_tensor("dbg_q2", [4, 8, 3, 256], F32,
                                         kind="ExternalOutput")
        self.t2s = nc.dram_tensor("t2s", [8, 128, 4096], F8, kind="Internal")
        if DEBUG:
            self.dbg_st = nc.dram_tensor("dbg_st", [128, 2120], F32,
                                         kind="ExternalOutput")

        S = {}

        def sb(name, shape, dtype, off):
            t = nc.alloc_sbuf_tensor_at(name, shape, dtype, offset=off)
            S[name] = t
            return t
        self.S = S

        C = 18432
        ident = sb("ident", [128, 128], F32, C + 0)
        onesm = sb("onesm", [128, 128], F32, C + 512)
        ones16 = sb("ones16", [128, 128], F16, C + 1024)
        onesbf = sb("onesbf", [128, 128], BF16, C + 1280)
        cneg = sb("cneg", [128, 1], F32, C + 1536)
        epsc = sb("epsc", [128, 1], F32, C + 1568)
        bq = sb("bq", [128, 24], F32, C + 1600)
        bo = sb("bo", [128, 8], F32, C + 1696)
        cw = sb("cw", [128, 96], F32, C + 1728)
        cb = sb("cb", [128, 32], F32, C + 2112)
        mw1c = sb("mw1c", [128, 1], F32, C + 2240)
        mw2ch = sb("mw2ch", [128, 1], F32, C + 2272)
        sw1 = sb("sw1", [128, 1], F32, C + 2304)
        sw2 = sb("sw2", [128, 1], F32, C + 2336)
        wacc = sb("wacc", [128, 16], F32, C + 2368)
        xsave = sb("xsave", [128, 8], F32, C + 2432)
        wstat = sb("wstat", [128, 8], F32, C + 2496)
        xr = sb("xr", [128, NDC * SEQ], F32, C + 2560)

        D = C + 2560 + 65536
        Q1W = 512
        NQ1C = SEQ // Q1W
        ring1 = sb("ring1", [128, 2 * 4096], F8, D)
        ring2 = sb("ring2", [128, 2 * 4096], F8, D + 8192)
        k1full = sb("k1full", [128, NDC * SEQ], BF16, D + 32768)
        hs = sb("hs", [128, 2 * FW], F32, D + 65536)
        cbuf = sb("cbuf", [128, 2 * FT], F32, D + 67616)
        hp = sb("hp", [128, NCC * FT], F32, D + 69664)
        k2i8 = sb("k2i8", [128, NCC * FT], I8, D + 102432)
        k2bf = sb("k2bf", [128, NCC * FT], BF16, D + 110624)
        self.q1stats = {}
        off = D + 69664
        for sn in ("mean_c", "tmp_c", "tmp2_c", "rstd_c", "m1_c", "a_c",
                   "q_c", "g1_c"):
            self.q1stats[sn] = sb(sn, [128, Q1W], F32, off)
            off += 2048
        xcq = sb("xcq", [128, NDC * Q1W], F32, D + 86048)
        k1i8q = sb("k1i8q", [128, NDC * Q1W], I8, D + 102432)
        xh16q = sb("xh16q", [128, NDC * Q1W], F16, D + 110624)
        xq16q = sb("xq16q", [128, NDC * Q1W], F16, D + 118816)
        self.stats = {}
        self.stats["m2_b"] = sb("m2_b", [128, FT], F32, D + 127104)
        self.stats["a2_b"] = sb("a2_b", [128, FT], F32, D + 128128)
        self.stats["s2_b"] = sb("s2_b", [128, FT], F32, D + 139936)
        self.stats["g2_b"] = sb("g2_b", [128, FT], F32, D + 140960)
        junkf = sb("junkf", [128, 128], F32, D + 141984)
        a_tm = sb("a_tm", [128, 4], F32, D + 137344)
        aT = sb("aT", [4, 128], F32, D + 137376)
        arow = sb("arow", [1, 512], F32, D + 137888)
        self.Q1W, self.NQ1C = Q1W, NQ1C
        assert D + 142496 <= 229344

        xnfull = sb("xnfull", [128, NDC * SEQ], BF16, D)
        qt_ = sb("qt_", [128, 4 * SEQ], BF16, D + 32768)
        kt_ = sb("kt_", [128, 4 * SEQ], BF16, D + 49152)
        vge = sb("vge", [128, NKC * 4 * 65], BF16, D + 65536)
        vgo = sb("vgo", [128, NKC * 4 * 128], BF16, D + 73856)
        osb = sb("osb", [128, NQT * 8 * QT], BF16, D + 90240)
        wqblk = sb("wqblk", [128, 2 * NDC * 128], BF16, D + 123008)
        vstage = sb("vstage", [128, QT], F32, D + 127104)
        AST = 256
        ah16 = sb("ah16", [128, NDC * AST], F16, D + 129152)
        aq16 = sb("aq16", [128, NDC * AST], F16, D + 133248)
        self.astats = {}
        off = D + 137344
        for sn in ("amean", "atmp", "atmp2", "arstd"):
            self.astats[sn] = sb(sn, [128, AST], F32, off)
            off += 1024
        assert off <= 229344, off
        pte = sb("pte", [128, QT], BF16, D + 129152)
        pto = sb("pto", [128, QT], BF16, D + 130176)
        zhl = sb("zhl", [128, 2 * QT], BF16, D + 131200)
        pte2 = sb("pte2", [128, QT], BF16, D + 133248)
        pto2 = sb("pto2", [128, QT], BF16, D + 134272)
        pteo = sb("pteo", [128, 2 * QT], BF16, D + 129152)
        pteo2 = sb("pteo2", [128, 2 * QT], BF16, D + 133248)
        zrow = sb("zrow", [128, QT], F32, D + 137344)
        zbsb = sb("zbsb", [128, QT], F32, D + 139392)
        wo_sb = sb("wo_sb", [128, NDC * DM], BF16, D + 65536)
        self.AST = AST
        wqf = sb("wqf", [128, 8 * 1024], F32, D)
        wqf2 = sb("wqf2", [128, 8 * 1024], F32, D + 69664)
        wqb = sb("wqb", [128, 8 * 1024], F8, D + 32768)
        wqhl = sb("wqhl", [128, 256], BF16, D + 40960)
        wqi = sb("wqi", [128, 8 * 1024], I8, D + 49152)

        pb01 = nc.alloc_psum_tensor("pb01", [128, 2 * PB], F32)

        class _BankView:
            def __init__(self, t, off):
                self.t, self.off = t, off

            def __getitem__(self, idx):
                rows, cols = idx
                return self.t[rows, cols.start + self.off:cols.stop + self.off]
        pb = ([_BankView(pb01, 0), _BankView(pb01, PB)] +
              [nc.alloc_psum_tensor("pb%d" % i, [128, PB], F32)
               for i in range(2, 8)])
        self.pb01 = pb01
        self.pb = pb

        mi = E("gpsimd", lambda g: g.memset(ident[:, :], 0.0), inc=("tG", 1))
        E("gpsimd", lambda g: g.affine_select(
            out=ident[:, :], in_=ident[:, :], compare_op=ALU.not_equal,
            fill=1.0, base=0, pattern=[[-1, 128]], channel_multiplier=1),
            waits=[mi])
        E("gpsimd", lambda g: g.memset(onesm[:, :], 1.0))
        E("gpsimd", lambda g: g.memset(ones16[:, :], 1.0))
        E("gpsimd", lambda g: g.memset(onesbf[:, :], 1.0))
        E("gpsimd", lambda g: g.memset(cneg[:, :], CSHIFT))
        E("gpsimd", lambda g: g.memset(epsc[:, :], 1e-5))
        E("gpsimd", lambda g: g.memset(wacc[:, :], 0.0))
        E("gpsimd", lambda g: g.memset(a_tm[:, :], 0.0))
        E("gpsimd", lambda g: g.memset(aT[:, :], 0.0))
        E("gpsimd", lambda g: g.memset(arow[:, :], 0.0))
        E("gpsimd", lambda g: g.memset(wstat[:, :], 0.0))
        E("gpsimd", lambda g: g.memset(junkf[:, :], 0.0), inc=("tG", 1))
        gp_const = ("tG", v("tG"))
        dma("scalar", "dC", bq[:, :], self.bqr[:, :])
        dma("scalar", "dC", bo[:, :], self.bor[:, :])
        dma("scalar", "dC", cw[:, :], self.cwr[:, :])
        w_bias = dma("scalar", "dC", cb[:, :], self.cbr[:, :])

        wqf2 = S["wqf2"]
        bufs = (wqf, wqf2)
        queues = (("sync", "dS"), ("gpsimd", "dG"))

        def load_w(i, waits=()):
            wi, g = divmod(i, 4)
            if wi == 0:
                src = bass.AP(self.w1t, g * 1024,
                              [[DH, 128], [128 * DH, 8], [1, 1024]])
            else:
                src = bass.AP(self.w2t, g * 1024 * DM,
                              [[DM, 128], [128 * DM, 8], [1, 1024]])
            eng, q = queues[i % 2]
            return dma(eng, q, bufs[i % 2][:, :], src, waits=waits)

        red_marks = {}
        for i in range(8):
            vl = load_w(i, waits=[red_marks[i - 2]] if i >= 2 else [])
            red_marks[i] = E("vector", lambda V, i=i: V.tensor_reduce(
                wacc[:, i:i + 1], bufs[i % 2][:, :], AX.X,
                ALU.add, apply_absolute_value=True),
                waits=[vl, gp_const], inc=("tV", 1))
        self.vflush()
        E("vector", lambda V: V.memset(wqhl[:, 0:256], 0.0))
        E("vector", lambda V: V.tensor_copy(wqhl[:, 0:8], wacc[:, 0:8]))
        self.vflush()
        E("vector", lambda V: V.tensor_tensor(
            wqhl[:, 128:136], wacc[:, 0:8], wqhl[:, 0:8], ALU.subtract))
        v_hl = self.vflush(inc=("tV", 1))
        E("tensor", lambda T: T.matmul(
            pb[7][:, 0:128], onesbf[:, :], wqhl[:, 0:128], start=True,
            stop=False), waits=self.wbank(7, [v_hl]))
        wt2 = E("tensor", lambda T: T.matmul(
            pb[7][:, 0:128], onesbf[:, :], wqhl[:, 128:256], start=False,
            stop=True), inc=("tT", 1))
        self.bank_ready[7] = wt2
        E("vector", lambda V: V.tensor_copy(wacc[:, 8:16], pb[7][:, 0:8]),
            waits=self.rbank(7, []))
        vfr = self.vflush(inc=("tV", 1))
        self.bank_free[7] = vfr
        E("vector", lambda V: V.tensor_reduce(
            wstat[:, 0:1], wacc[:, 8:12], AX.X, ALU.add))
        self.vflush()
        E("vector", lambda V: V.tensor_reduce(
            wstat[:, 1:2], wacc[:, 12:16], AX.X, ALU.add))
        self.vflush()
        E("vector", lambda V: V.tensor_scalar(
            wstat[:, 2:4], wstat[:, 0:2], 1.0 / (DM * DH), None, ALU.mult))
        self.vflush()
        E("vector", lambda V: V.tensor_scalar(
            wstat[:, 4:6], wstat[:, 2:4], 1e-5, None, ALU.max))
        self.vflush()
        E("vector", lambda V: V.tensor_scalar(
            mw1c[:, 0:1], wstat[:, 4:5], 1.0 / 127.0, None, ALU.mult))
        E("vector", lambda V: V.tensor_scalar(
            mw2ch[:, 0:1], wstat[:, 5:6], 0.5 / 127.0, None, ALU.mult))
        self.vflush()
        E("vector", lambda V: V.reciprocal(sw1[:, 0:1], wstat[:, 4:5]))
        self.vflush()
        E("vector", lambda V: V.reciprocal(sw2[:, 0:1], wstat[:, 5:6]))
        v_repl = self.vflush(inc=("tV", 1))

        cast_marks = {}
        prev_store = None
        self.scr_marks = {}
        for i in range(8):
            swc = sw1 if i < 4 else sw2
            if i >= 2:
                lw = [cast_marks[i - 2]]
            else:
                lw = [red_marks[6 + (i % 2)]]
            vl = load_w(i, waits=lw)
            buf = bufs[i % 2]
            E("vector", lambda V, swc=swc, buf=buf: V.tensor_scalar(
                buf[:, :], buf[:, :], swc[:, 0:1], 1.4, ALU.mult, ALU.min),
                waits=[vl, v_repl])
            vclip = E("vector", lambda V, buf=buf: V.tensor_scalar(
                buf[:, :], buf[:, :], -1.4, None, ALU.max), inc=("tV", 1))
            cast_marks[i] = E("scalar", lambda Sc, buf=buf: Sc.activation(
                wqi[:, :], buf[:, :], AF.Copy), waits=[vclip], inc=("tS", 1))
            vb = E("vector", lambda V: V.tensor_copy(wqb[:, :], wqi[:, :]),
                   waits=[cast_marks[i]] + ([prev_store] if prev_store else []),
                   inc=("tV", 1))
            wi, g = divmod(i, 4)
            if wi == 0:
                for u in range(2):
                    dst = bass.AP(self.t1s, (2 * g + u) * 128 * 4096,
                                  [[4096, 128], [512, 8], [1, 512]])
                    srcb = bass.AP(wqb, u * 512,
                                   [[8192, 128], [1024, 8], [1, 512]])
                    prev_store = dma("scalar", "dC", dst, srcb, waits=[vb])
                    self.scr_marks[(1, 2 * g + u)] = prev_store
            else:
                dst = bass.AP(self.t2s, (2 * g) * 128 * 4096,
                              [[4096, 128], [128 * 4096, 2], [1, 4096]])
                prev_store = dma("scalar", "dC", dst, wqb[:, :], waits=[vb])
                self.scr_marks[(2, 2 * g)] = prev_store
                self.scr_marks[(2, 2 * g + 1)] = prev_store
        self.dc_scr = ("dC", v("dC"))
        self.ring_prev = {}
        self.ring_marks = {}

        for seq in (0, 1):
            self.phase_barrier()
            self.emit_F(seq, first=True, store_out=(PHASES == "f1"))
            if PHASES == "f1":
                continue
            self.phase_barrier()
            self.emit_A(seq)
            if PHASES == "f1a":
                self.phase_barrier()
                dstr = bass.AP(self.ofm, seq * DM * SEQ,
                               [[SEQ, 128], [128 * SEQ, NDC], [1, SEQ]])
                dma("scalar", "dC", dstr, S["xr"][:, :],
                    waits=self.barrier_waits())
                continue
            self.phase_barrier()
            self.emit_F(seq, first=False, store_out=True)

        if DEBUG:
            bwf = self.barrier_waits()
            dma("scalar", "dC", self.dbg_st[:, :], S["dbgstage"][:, :], waits=bwf)
        fin = [("dC", v("dC")), ("dS", v("dS")), ("dG", v("dG"))]
        for e in ENGS:
            self.E(e, lambda eng: eng.nop(), fin)

    def repl_max(self, m_src, c0, W, dst_b, v_wait):
        E, v, pb = self.E, self.v, self.pb
        S = self.S
        ident, onesm = S["ident"], S["onesm"]
        a_tm, aT, arow = S["a_tm"], S["aT"], S["arow"]
        nblk = (W + 127) // 128
        for b in range(nblk):
            cc0 = c0 + b * 128
            cwd = min(128, c0 + W - cc0)
            bk = 6 + (b % 2)
            tb = E("tensor", lambda T, cc0=cc0, cwd=cwd, bk=bk: T.transpose(
                pb[bk][0:cwd, 0:128],
                m_src[:, cc0:cc0 + cwd], ident[:, :]),
                waits=self.wbank(bk, [v_wait] if b == 0 else []),
                inc=("tT", 1))
            self.bank_ready[bk] = tb
            vr = E("vector", lambda V, cwd=cwd, b=b, bk=bk: V.tensor_reduce(
                a_tm[0:cwd, b:b + 1], pb[bk][0:cwd, 0:128],
                AX.X, ALU.max),
                waits=self.rbank(bk, []), inc=("tV", 1))
            self.bank_free[bk] = vr
            self.vflush()
        v_red = self.vflush(inc=("tV", 1))
        t_at = E("tensor", lambda T, nblk=nblk: T.transpose(
            pb[7][0:nblk, 0:128], a_tm[:, 0:nblk], ident[:, :]),
            waits=self.wbank(7, [v_red]), inc=("tT", 1))
        self.bank_ready[7] = t_at
        E("vector", lambda V, nblk=nblk: V.tensor_copy(
            aT[0:nblk, :], pb[7][0:nblk, 0:128]),
            waits=self.rbank(7, []))
        v_at = self.vflush(inc=("tV", 1))
        self.bank_free[7] = v_at
        full = (W // 128) * 128
        g1 = g2 = None
        if full > 0:
            dsta = bass.AP(arow, c0, [[FW, 1], [128, W // 128], [1, 128]])
            g1 = self.dma("gpsimd", "dG", dsta, aT[0:W // 128, :], waits=[v_at])
        rem = W - full
        if rem > 0:
            dsta = bass.AP(arow, c0 + full, [[FW, 1], [1, rem]])
            g2 = self.dma("gpsimd", "dG", dsta, aT[W // 128:W // 128 + 1, 0:rem],
                          waits=[v_at] if full == 0 else [])
        dg = ("dG", v("dG"))
        t_rep = E("tensor", lambda T, c0=c0, W=W: T.matmul(
            pb[7][:, 0:W], onesm[0:1, :], arow[:, c0:c0 + W],
            start=True, stop=True),
            waits=self.wbank(7, [dg, v_at]), inc=("tT", 1))
        self.bank_ready[7] = t_rep
        v_ab = E("vector", lambda V, c0=c0, W=W, dst_b=dst_b: V.tensor_copy(
            dst_b[:, c0:c0 + W], pb[7][:, 0:W]),
            waits=self.rbank(7, []), inc=("tV", 1))
        self.bank_free[7] = v_ab
        return v_ab

    def emit_F(self, seq, first, store_out):
        E, v, dma, pb, S = self.E, self.v, self.dma, self.pb, self.S
        vx = None
        if first:
            src = bass.AP(self.xfm, seq * DM * SEQ,
                          [[SEQ, 128], [128 * SEQ, NDC], [1, SEQ]])
            bwx = self.barrier_waits() if seq else []
            vx = dma("scalar", "dC", S["xr"][:, :], src, waits=bwx)
        self.q1_marks = {}
        self.q1_xr_guard = None
        self.pending_mm2 = None
        self.cbuf_prev = None
        self.conv_prev = None
        self.evac_prev = None
        self.k2_prev = {}
        self.mm2_mark = None
        self.fidx = getattr(self, 'fidx', -1) + 1
        for ch in range(self.NQ1C):
            self.emit_Q1(ch, vx if ch == 0 else None)
        for t in range(NFT):
            self._F_mm1(seq, t)
            if self.pending_mm2 is not None:
                self.pending_mm2()
                self.pending_mm2 = None
            self._F_quant2(t)
            self._F_mm2_defer(seq, t, store_out)
        self.pending_mm2()
        self.pending_mm2 = None

    def _xr(self, dc, c, w):
        return self.S["xr"][:, dc * SEQ + c: dc * SEQ + c + w]

    def emit_Q1(self, ch, vx):
        E, v, pb, S = self.E, self.v, self.pb, self.S
        q = self.q1stats
        Q1W = self.Q1W
        q0 = ch * Q1W
        xh16q, xq16q, xcq, k1i8q = (S["xh16q"], S["xq16q"], S["xcq"],
                                    S["k1i8q"])
        k1full = S["k1full"]
        ones16, epsc, mw1c = S["ones16"], S["epsc"], S["mw1c"]
        qs = lambda n: q[n][:, 0:Q1W]
        for dc in range(NDC):
            E("scalar", lambda Sc, dc=dc: Sc.activation(
                xh16q[:, dc * Q1W:(dc + 1) * Q1W], self._xr(dc, q0, Q1W),
                AF.Copy),
                waits=[vx] if (dc == 0 and vx is not None) else [])
        for dc in range(NDC):
            E("scalar", lambda Sc, dc=dc: Sc.activation(
                xq16q[:, dc * Q1W:(dc + 1) * Q1W],
                xh16q[:, dc * Q1W:(dc + 1) * Q1W], AF.Square),
                inc=("tS", 1) if dc == NDC - 1 else None)
        s_xh = ("tS", v("tS"))
        self.q1_xr_guard = s_xh
        for dc in range(NDC):
            E("tensor", lambda T, dc=dc: T.matmul(
                pb[6][:, 0:Q1W], ones16[:, :],
                xh16q[:, dc * Q1W:(dc + 1) * Q1W],
                start=(dc == 0), stop=(dc == NDC - 1)),
                waits=self.wbank(6, [s_xh]) if dc == 0 else [])
        for dc in range(NDC):
            E("tensor", lambda T, dc=dc: T.matmul(
                pb[7][:, 0:Q1W], ones16[:, :],
                xq16q[:, dc * Q1W:(dc + 1) * Q1W],
                start=(dc == 0), stop=(dc == NDC - 1)),
                waits=self.wbank(7, []) if dc == 0 else [],
                inc=("tT", 1) if dc == NDC - 1 else None)
        t_sums = ("tT", v("tT"))
        self.bank_ready[6] = t_sums
        self.bank_ready[7] = t_sums
        mw_ = self.rbank(6, [])
        if self.q1_marks:
            mw_.append(self.q1_marks[max(self.q1_marks)])
        E("scalar", lambda Sc: Sc.activation(
            qs("mean_c"), pb[6][:, 0:Q1W], AF.Copy, scale=1.0 / DM),
            waits=mw_)
        s_m = E("scalar", lambda Sc: Sc.activation(
            qs("tmp_c"), pb[7][:, 0:Q1W], AF.Copy, scale=1.0 / DM),
            waits=self.rbank(7, []), inc=("tS", 1))
        self.bank_free[6] = s_m
        self.bank_free[7] = s_m
        E("vector", lambda V: V.tensor_tensor(
            qs("tmp2_c"), qs("mean_c"), qs("mean_c"), ALU.mult), waits=[s_m])
        E("vector", lambda V: V.tensor_tensor(
            qs("tmp_c"), qs("tmp_c"), qs("tmp2_c"), ALU.subtract))
        v_var = E("vector", lambda V: V.tensor_scalar(
            qs("tmp_c"), qs("tmp_c"), 0.0, None, ALU.max), inc=("tV", 1))
        E("scalar", lambda Sc: Sc.activation(
            qs("tmp2_c"), qs("tmp_c"), AF.Ln, bias=epsc[:, 0:1]),
            waits=[v_var])
        s_r = E("scalar", lambda Sc: Sc.activation(
            qs("rstd_c"), qs("tmp2_c"), AF.Exp, scale=-0.5), inc=("tS", 1))
        for dc in range(NDC):
            E("vector", lambda V, dc=dc, q0=q0: V.tensor_tensor(
                xcq[:, dc * Q1W:(dc + 1) * Q1W], self._xr(dc, q0, Q1W),
                qs("mean_c"), ALU.subtract),
                waits=[s_r] if dc == 0 else [])
        v_m1w = E("vector", lambda V: V.tensor_reduce(
            qs("m1_c"),
            bass.AP(xcq, 0, [[NDC * Q1W, 128], [1, Q1W], [Q1W, NDC]]),
            AX.X, ALU.max, apply_absolute_value=True), inc=("tV", 1))
        self.repl_max(q["m1_c"], 0, Q1W, q["a_c"], v_m1w)
        E("vector", lambda V: V.tensor_tensor(
            qs("tmp_c"), qs("a_c"), qs("rstd_c"), ALU.mult))
        E("vector", lambda V: V.tensor_scalar(
            qs("tmp_c"), qs("tmp_c"), 1e-5, None, ALU.max))
        vq_r = E("vector", lambda V: V.reciprocal(
            qs("tmp2_c"), qs("tmp_c")), inc=("tV", 1))
        E("vector", lambda V: V.scalar_tensor_tensor(
            qs("q_c"), qs("tmp2_c"), 127.0, qs("rstd_c"),
            ALU.mult, ALU.mult), waits=[vq_r])
        E("vector", lambda V: V.tensor_scalar(
            qs("g1_c"), qs("tmp_c"), mw1c[:, 0:1], None, ALU.mult))
        for dc in range(NDC):
            E("vector", lambda V, dc=dc: V.tensor_tensor(
                k1i8q[:, dc * Q1W:(dc + 1) * Q1W],
                xcq[:, dc * Q1W:(dc + 1) * Q1W], qs("q_c"), ALU.mult))
        for dc in range(NDC):
            E("vector", lambda V, dc=dc, q0=q0: V.tensor_tensor(
                k1full[:, dc * SEQ + q0:dc * SEQ + q0 + Q1W],
                k1i8q[:, dc * Q1W:(dc + 1) * Q1W], qs("g1_c"), ALU.mult),
                inc=("tV", 1) if dc == NDC - 1 else None)
        self.q1_marks[ch] = ("tV", v("tV"))
        if DBGS and DBGS_Q1:
            dmark = None
            for j, nm in enumerate(("q_c", "g1_c", "a_c", "rstd_c", "mean_c")):
                dmark = self.dma("scalar", "dC",
                                 self.dbg_q1[self.fidx, ch, j, :],
                                 q[nm][0:1, 0:Q1W],
                                 waits=[self.q1_marks[ch]] if j == 0 else [])
            self.q1_dump = dmark
            E("vector", lambda V: V.engine_nop(), waits=[dmark])

    def _F_mm1(self, seq, t):
        E, v, dma, pb, S = self.E, self.v, self.dma, self.pb, self.S
        a0 = t * FT
        lo = max(a0 - 1, 0)
        hi = min(a0 + FT + 1, SEQ)
        W = hi - lo
        c0 = lo - (a0 - 1)
        hs, cbuf, hp = S["hs"], S["cbuf"], S["hp"]
        k1full, ring1 = S["k1full"], S["ring1"]
        cw, cb = S["cw"], S["cb"]
        need_ch = (a0 + FT) // self.Q1W
        v_k1 = self.q1_marks[min(need_ch, self.NQ1C - 1)]
        if t == 0:
            for sl in range(2):
                E("vector", lambda V, sl=sl: V.memset(
                    hs[:, sl * FW:sl * FW + 1], 0.0))
            self.vflush()
        if t == NFT - 1:
            for sl in range(2):
                E("vector", lambda V, sl=sl: V.memset(
                    hs[:, sl * FW + FW - 1:sl * FW + FW], 0.0))
            self.vflush()
        conv_done = {}
        silu_done = {}
        evac_done = {}
        pend_silu = None
        for cc in range(NCC):
            g = cc // 4
            slot = g % 2
            sl = cc % 2
            if cc % 4 == 0:
                self._ring_load(1, g, slot)
            bank = 4 + sl
            waits = []
            if cc == 0:
                waits.append(v_k1)
            if cc % 4 == 0:
                waits.append(self.ring_marks[("r1", slot)])
            if cc >= 2:
                waits.append(evac_done[cc - 2])
            elif self.evac_prev is not None:
                waits.append(self.evac_prev[cc])
            for dc in range(NDC):
                E("tensor", lambda T, dc=dc, cc=cc, slot=slot, bank=bank,
                  lo=lo, W=W, c0=c0: T.matmul(
                      pb[bank][:, c0:c0 + W],
                      ring1[:, slot * 4096 + dc * 512 + (cc % 4) * 128:
                            slot * 4096 + dc * 512 + (cc % 4) * 128 + 128],
                      k1full[:, dc * SEQ + lo:dc * SEQ + lo + W],
                      start=(dc == 0), stop=(dc == NDC - 1)),
                  waits=waits if dc == 0 else [],
                  inc=("tT", 1) if dc == NDC - 1 else None)
            t_mm1 = ("tT", v("tT"))
            if cc % 4 == 3:
                self.ring_prev[("r1", slot)] = t_mm1
            hw_ = [t_mm1]
            if cc >= 2:
                hw_.append(conv_done[cc - 2])
            elif self.conv_prev is not None:
                hw_.append(self.conv_prev[cc])
            evac_done[cc] = E("scalar", lambda Sc, sl=sl, bank=bank, c0=c0,
                              W=W: Sc.activation(
                hs[:, sl * FW + c0:sl * FW + c0 + W],
                pb[bank][:, c0:c0 + W], AF.Copy), waits=hw_, inc=("tS", 1))
            cwait = [evac_done[cc]]
            if cc >= 2:
                cwait.append(silu_done[cc - 2])
            elif self.cbuf_prev is not None:
                cwait.append(self.cbuf_prev[cc])
            cbo = sl * FT
            E("vector", lambda V, cc=cc, cbo=cbo, sl=sl: V.tensor_scalar(
                cbuf[:, cbo:cbo + FT], hs[:, sl * FW:sl * FW + FT],
                cw[:, cc * 3:cc * 3 + 1], cb[:, cc:cc + 1],
                ALU.mult, ALU.add), waits=cwait)
            E("vector", lambda V, cc=cc, cbo=cbo, sl=sl: V.scalar_tensor_tensor(
                cbuf[:, cbo:cbo + FT], hs[:, sl * FW + 1:sl * FW + FT + 1],
                cw[:, cc * 3 + 1:cc * 3 + 2], cbuf[:, cbo:cbo + FT],
                ALU.mult, ALU.add))
            vc = E("vector", lambda V, cc=cc, cbo=cbo, sl=sl:
                   V.scalar_tensor_tensor(
                cbuf[:, cbo:cbo + FT], hs[:, sl * FW + 2:sl * FW + FT + 2],
                cw[:, cc * 3 + 2:cc * 3 + 3], cbuf[:, cbo:cbo + FT],
                ALU.mult, ALU.add), inc=("tV", 1))
            conv_done[cc] = vc
            if pend_silu is not None:
                pcc, pvc = pend_silu
                sw_ = [pvc]
                if pcc in self.k2_prev:
                    sw_.append(self.k2_prev[pcc])
                silu_done[pcc] = E("scalar", lambda Sc, pcc=pcc: Sc.activation(
                    hp[:, pcc * FT:pcc * FT + FT],
                    cbuf[:, (pcc % 2) * FT:(pcc % 2) * FT + FT], AF.Silu),
                    waits=sw_, inc=("tS", 1))
            pend_silu = (cc, vc)
        pcc, pvc = pend_silu
        sw_ = [pvc]
        if pcc in self.k2_prev:
            sw_.append(self.k2_prev[pcc])
        silu_done[pcc] = E("scalar", lambda Sc, pcc=pcc: Sc.activation(
            hp[:, pcc * FT:pcc * FT + FT],
            cbuf[:, (pcc % 2) * FT:(pcc % 2) * FT + FT], AF.Silu),
            waits=sw_, inc=("tS", 1))
        self.s_silu = ("tS", v("tS"))
        self.cbuf_prev = (silu_done[NCC - 2], silu_done[NCC - 1])
        self.conv_prev = (conv_done[NCC - 2], conv_done[NCC - 1])
        self.evac_prev = (evac_done[NCC - 2], evac_done[NCC - 1])
        self.silu_done_t = silu_done

    def _F_quant2(self, t):
        E, v, pb, S = self.E, self.v, self.pb, self.S
        st = self.stats
        hp, k2i8, k2bf = S["hp"], S["k2i8"], S["k2bf"]
        mw2ch = S["mw2ch"]
        v_m2 = E("vector", lambda V: V.tensor_reduce(
            st["m2_b"][:, 0:FT],
            bass.AP(hp, 0, [[NCC * FT, 128], [1, FT], [FT, NCC]]),
            AX.X, ALU.max, apply_absolute_value=True),
            waits=[self.s_silu], inc=("tV", 1))
        self.repl_max(st["m2_b"], 0, FT, st["a2_b"], v_m2)
        E("vector", lambda V: V.tensor_scalar(
            st["a2_b"][:, 0:FT], st["a2_b"][:, 0:FT], 1e-5, None, ALU.max))
        vq2_r = E("vector", lambda V: V.reciprocal(
            st["m2_b"][:, 0:FT], st["a2_b"][:, 0:FT]), inc=("tV", 1))
        E("vector", lambda V: V.tensor_scalar(
            st["s2_b"][:, 0:FT], st["m2_b"][:, 0:FT], 127.0, None, ALU.mult),
            waits=[vq2_r])
        E("vector", lambda V: V.tensor_scalar(
            st["g2_b"][:, 0:FT], st["a2_b"][:, 0:FT], mw2ch[:, 0:1], None,
            ALU.mult), inc=("tV", 1))
        v_q2 = ("tV", v("tV"))
        if DBGS:
            dmark = None
            for j, nm in enumerate(("s2_b", "g2_b", "a2_b")):
                dmark = self.dma("scalar", "dC",
                                 self.dbg_q2[self.fidx, t, j, :],
                                 st[nm][0:1, 0:FT],
                                 waits=[v_q2] if j == 0 else [])
            E("vector", lambda V: V.engine_nop(), waits=[dmark])
        self.k2v = {}
        k2_cur = {}
        for cc in range(NCC):
            E("vector", lambda V, cc=cc: V.tensor_tensor(
                k2i8[:, cc * FT:cc * FT + FT], hp[:, cc * FT:cc * FT + FT],
                st["s2_b"][:, 0:FT], ALU.mult),
                waits=([v_q2] + ([self.mm2_mark] if self.mm2_mark else []))
                if cc == 0 else [], inc=("tV", 1))
            k2_cur[cc] = ("tV", v("tV"))
            E("scalar", lambda Sc, cc=cc: Sc.activation(
                k2bf[:, cc * FT:cc * FT + FT], k2i8[:, cc * FT:cc * FT + FT],
                AF.Copy), waits=[k2_cur[cc]], inc=("tS", 1))
            self.k2v[cc] = ("tS", v("tS"))
        self.k2_prev = k2_cur

    def _F_mm2_defer(self, seq, t, store_out):
        E, v, dma, pb, S = self.E, self.v, self.dma, self.pb, self.S
        st = self.stats
        k2bf, ring2 = S["k2bf"], S["ring2"]
        a0 = t * FT
        k2v = dict(self.k2v)
        xr_guard = self.q1_xr_guard

        def doit():
            for cc in range(NCC):
                g = cc // 4
                slot = g % 2
                if cc % 4 == 0:
                    self._ring_load(2, g, slot)
                waits = [k2v[cc]]
                if cc % 4 == 0:
                    waits.append(self.ring_marks[("r2", slot)])
                if cc == 0:
                    for bk in range(4):
                        waits += self.wbank(bk, [])
                for dm in range(NDC):
                    bank = dm // 2
                    half = (dm % 2) * FT
                    E("tensor", lambda T, cc=cc, dm=dm, bank=bank, half=half,
                      slot=slot: T.matmul(
                          pb[bank][:, half:half + FT],
                          ring2[:, slot * 4096 + (cc % 4) * 1024 + dm * 128:
                                slot * 4096 + (cc % 4) * 1024 + dm * 128 + 128],
                          k2bf[:, cc * FT:cc * FT + FT],
                          start=(cc == 0 and dm % 2 == 0),
                          stop=(cc == NCC - 1 and dm % 2 == 1)),
                      waits=waits if dm == 0 else [],
                      inc=("tT", 1) if (dm == NDC - 1 and cc % 4 == 3) else None)
                if cc % 4 == 3:
                    self.ring_prev[("r2", slot)] = ("tT", v("tT"))
            t_mm2 = ("tT", v("tT"))
            self.mm2_mark = t_mm2
            for bk in range(4):
                self.bank_ready[bk] = t_mm2
            for dm in range(NDC):
                bank = dm // 2
                half = (dm % 2) * FT
                ew = self.rbank(bank, []) if dm % 2 == 0 else []
                if dm == 0 and xr_guard is not None:
                    ew = ew + [xr_guard]
                E("vector", lambda V, bank=bank, half=half: V.tensor_tensor(
                    st["a2_b"][:, 0:FT], pb[bank][:, half:half + FT],
                    st["g2_b"][:, 0:FT], ALU.mult), waits=ew)
                E("vector", lambda V, dm=dm: V.tensor_tensor(
                    self._xr(dm, a0, FT), self._xr(dm, a0, FT),
                    st["a2_b"][:, 0:FT], ALU.add),
                    inc=("tV", 1) if dm == NDC - 1 else None)
            v_tile = ("tV", v("tV"))
            for bk in range(4):
                self.bank_free[bk] = v_tile
            if store_out:
                dstr = bass.AP(self.ofm, seq * DM * SEQ + a0,
                               [[SEQ, 128], [128 * SEQ, NDC], [1, FT]])
                srcr = bass.AP(S["xr"], a0,
                               [[NDC * SEQ, 128], [SEQ, NDC], [1, FT]])
                dma("scalar", "dC", dstr, srcr, waits=[v_tile])
        self.pending_mm2 = doit

    def _ring_load(self, which, g, slot):
        S, dma = self.S, self.dma
        key = ("r%d" % which, slot)
        waits = [self.scr_marks[(which, g % 8)]]
        if key in self.ring_prev:
            waits.append(self.ring_prev[key])
        eng, q = ("sync", "dS")
        if which == 1:
            src = bass.AP(self.t1s, (g % 8) * 128 * 4096,
                          [[4096, 128], [1, 4096]])
            dst = S["ring1"][:, slot * 4096:slot * 4096 + 4096]
        else:
            src = bass.AP(self.t2s, (g % 8) * 128 * 4096,
                          [[4096, 128], [1, 4096]])
            dst = S["ring2"][:, slot * 4096:slot * 4096 + 4096]
        self.ring_marks[key] = dma(eng, q, dst, src, waits=waits)

    def emit_A(self, seq):
        E, v, dma, pb, S = self.E, self.v, self.dma, self.pb, self.S
        ast = self.astats
        AST = self.AST
        at = lambda n: ast[n][:, 0:AST]
        qt_, kt_, vge, vgo = S["qt_"], S["kt_"], S["vge"], S["vgo"]
        osb, wqblk, vstage = S["osb"], S["wqblk"], S["vstage"]
        xnfull, ah16, aq16 = S["xnfull"], S["ah16"], S["aq16"]
        wo_sb = S["wo_sb"]
        ones16, onesm, onesbf, ident = (S["ones16"], S["onesm"], S["onesbf"],
                                        S["ident"])
        epsc, cneg, bq, bo = S["epsc"], S["cneg"], S["bq"], S["bo"]
        pte, pto, zrow, zhl, zbsb = (S["pte"], S["pto"], S["zrow"], S["zhl"],
                                     S["zbsb"])
        pte2, pto2 = S["pte2"], S["pto2"]
        pteo, pteo2 = S["pteo"], S["pteo2"]
        bw = self.barrier_waits()
        self.vstage_free = None
        self.pv_mark = {}
        self.pending_bcast = None

        for kc in range(NKC):
            for p in range(4):
                eo = (kc * 4 + p) * 65
                oo = (kc * 4 + p) * 128
                E("gpsimd", lambda g, eo=eo: g.memset(
                    vge[:, eo + 64:eo + 65], 1.0),
                    waits=bw if (kc == 0 and p == 0) else [])
                E("gpsimd", lambda g, oo=oo: g.memset(vgo[:, oo:oo + 1], 1.0))
                E("gpsimd", lambda g, oo=oo: g.memset(
                    vgo[:, oo + 1:oo + 64], 0.0))
        E("gpsimd", lambda g: g.memset(zrow[0:1, 0:1], 0.0), inc=("tG", 1))
        g_aug = ("tG", v("tG"))

        for t8 in range(SEQ // AST):
            t0 = t8 * AST
            for dc in range(NDC):
                E("vector", lambda V, dc=dc, t0=t0: V.tensor_copy(
                    ah16[:, dc * AST:dc * AST + AST], self._xr(dc, t0, AST)),
                    waits=bw if (t8 == 0 and dc == 0) else [])
            for dc in range(NDC):
                E("vector", lambda V, dc=dc: V.tensor_tensor(
                    aq16[:, dc * AST:dc * AST + AST],
                    ah16[:, dc * AST:dc * AST + AST],
                    ah16[:, dc * AST:dc * AST + AST], ALU.mult),
                    inc=("tV", 1) if dc == NDC - 1 else None)
            v16 = ("tV", v("tV"))
            for dc in range(NDC):
                E("tensor", lambda T, dc=dc: T.matmul(
                    pb[6][:, 0:AST], ones16[:, :], ah16[:, dc * AST:dc * AST + AST],
                    start=(dc == 0), stop=(dc == NDC - 1)),
                    waits=self.wbank(6, [v16]) if dc == 0 else [])
            for dc in range(NDC):
                E("tensor", lambda T, dc=dc: T.matmul(
                    pb[7][:, 0:AST], ones16[:, :], aq16[:, dc * AST:dc * AST + AST],
                    start=(dc == 0), stop=(dc == NDC - 1)),
                    waits=self.wbank(7, []) if dc == 0 else [],
                    inc=("tT", 1) if dc == NDC - 1 else None)
            tsum = ("tT", v("tT"))
            self.bank_ready[6] = tsum
            self.bank_ready[7] = tsum
            E("scalar", lambda Sc: Sc.activation(
                at("amean"), pb[6][:, 0:AST], AF.Copy, scale=1.0 / DM),
                waits=self.rbank(6, []))
            sm = E("scalar", lambda Sc: Sc.activation(
                at("atmp"), pb[7][:, 0:AST], AF.Copy, scale=1.0 / DM),
                waits=self.rbank(7, []), inc=("tS", 1))
            self.bank_free[6] = sm
            self.bank_free[7] = sm
            E("vector", lambda V: V.tensor_tensor(
                at("atmp2"), at("amean"), at("amean"), ALU.mult), waits=[sm])
            E("vector", lambda V: V.tensor_tensor(
                at("atmp"), at("atmp"), at("atmp2"), ALU.subtract))
            vv = E("vector", lambda V: V.tensor_scalar(
                at("atmp"), at("atmp"), 0.0, None, ALU.max), inc=("tV", 1))
            E("scalar", lambda Sc: Sc.activation(
                at("atmp2"), at("atmp"), AF.Ln, bias=epsc[:, 0:1]), waits=[vv])
            sr = E("scalar", lambda Sc: Sc.activation(
                at("arstd"), at("atmp2"), AF.Exp, scale=-0.5), inc=("tS", 1))
            for dc in range(NDC):
                E("vector", lambda V, dc=dc, t0=t0: V.tensor_tensor(
                    at("atmp2"), self._xr(dc, t0, AST), at("amean"),
                    ALU.subtract),
                    waits=[sr] if dc == 0 else [])
                E("vector", lambda V, dc=dc, t0=t0: V.tensor_tensor(
                    xnfull[:, dc * SEQ + t0:dc * SEQ + t0 + AST], at("atmp2"),
                    at("arstd"), ALU.mult),
                    inc=("tV", 1) if dc == NDC - 1 else None)
        v_xn = ("tV", v("tV"))
        E("gpsimd", lambda g: g.memset(zbsb[:, :], 1.0),
          waits=[v_xn], inc=("tG", 1))
        self.zbsb_claim = ("tG", v("tG"))

        wq_used = {0: None, 1: None}
        exp_done = {0: None, 1: None}
        for hg in range(2):
            fcs = ([hg * 4 + j for j in range(4)] +
                   [8 + hg * 4 + j for j in range(4)] +
                   [16 + hg * 4 + j for j in range(4)])
            for i, fc in enumerate(fcs):
                slot = i % 2
                srcw = bass.AP(self.wqkv_r, fc * DM * 128,
                               [[128, 128], [128 * 128, NDC], [1, 128]])
                lw = [v_xn] if (hg == 0 and i == 0) else []
                if wq_used[slot] is not None:
                    lw.append(wq_used[slot])
                vld = dma("gpsimd", "dG",
                          wqblk[:, slot * 1024:slot * 1024 + 1024], srcw,
                          waits=lw)
                role = "q" if fc < 8 else ("k" if fc < 16 else "v")
                lc = fc - (0 if fc < 8 else (8 if fc < 16 else 16)) - hg * 4
                for tt in range(NQT):
                    t0 = tt * QT
                    bank = 4 + ((i * NQT + tt) % 2)
                    waits = [vld, v_xn] if tt == 0 else []
                    waits = self.wbank(bank, waits)
                    for dc in range(NDC):
                        E("tensor", lambda T, dc=dc, slot=slot, bank=bank,
                          t0=t0: T.matmul(
                              pb[bank][:, 0:QT],
                              wqblk[:, slot * 1024 + dc * 128:
                                    slot * 1024 + dc * 128 + 128],
                              xnfull[:, dc * SEQ + t0:dc * SEQ + t0 + QT],
                              start=(dc == 0), stop=(dc == NDC - 1)),
                            waits=waits if dc == 0 else [],
                            inc=("tT", 1) if dc == NDC - 1 else None)
                    tq = ("tT", v("tT"))
                    self.bank_ready[bank] = tq
                    if tt == NQT - 1:
                        wq_used[slot] = tq
                    if role == "q":
                        dstp = qt_[:, lc * SEQ + t0:lc * SEQ + t0 + QT]
                    elif role == "k":
                        dstp = kt_[:, lc * SEQ + t0:lc * SEQ + t0 + QT]
                    else:
                        dstp = vstage[:, 0:QT]
                    evw = self.rbank(bank, [])
                    if role == "v" and self.vstage_free is not None:
                        evw.append(self.vstage_free)
                    sev = E("vector", lambda V, fc=fc, dstp=dstp, bank=bank:
                            V.tensor_scalar(dstp, pb[bank][:, 0:QT],
                                            bq[:, fc:fc + 1], None, ALU.add),
                            waits=evw, inc=("tV", 1))
                    self.bank_free[bank] = sev
                    if role == "v":
                        for tb in range(QT // 128):
                            kc = (t0 + tb * 128) // 128
                            tt2 = E("tensor", lambda T, tb=tb: T.transpose(
                                pb[7][:, 0:128],
                                vstage[:, tb * 128:tb * 128 + 128],
                                ident[:, :]),
                                waits=self.wbank(7, [sev, g_aug]
                                                 if tb == 0 else []),
                                inc=("tT", 1))
                            self.bank_ready[7] = tt2
                            E("vector", lambda V, kc=kc, lc=lc: V.tensor_copy(
                                vge[:, (kc * 4 + lc) * 65:(kc * 4 + lc) * 65 + 64],
                                pb[7][:, 0:64]), waits=self.rbank(7, []))
                            vt2 = E("vector", lambda V, kc=kc, lc=lc:
                                    V.tensor_copy(
                                        vgo[:, (kc * 4 + lc) * 128 + 64:
                                            (kc * 4 + lc) * 128 + 128],
                                        pb[7][:, 64:128]), inc=("tV", 1))
                            self.bank_free[7] = vt2
                            if tb == QT // 128 - 1:
                                self.vstage_free = tt2
            v_sweep = ("tV", v("tV"))
            s_sweep = ("tS", v("tS"))


            def emit_pv(prev, p):
                se, so, kc = prev
                par = kc % 2
                wpe = self.wbank(2, [se]) if kc == 0 else [se]
                E("tensor", lambda T, p=p, kc=kc, par=par: T.matmul(
                    pb[2][0:65, 0:QT],
                    vge[:, (kc * 4 + p) * 65:(kc * 4 + p) * 65 + 65],
                    (pteo if par == 0 else pteo2)[:, 0:QT],
                    start=(kc == 0), stop=(kc == NKC - 1)),
                    waits=wpe)
                wpo = self.wbank(3, [so]) if kc == 0 else [so]
                E("tensor", lambda T, p=p, kc=kc, par=par: T.matmul(
                    pb[3][0:128, 0:QT],
                    vgo[:, (kc * 4 + p) * 128:(kc * 4 + p) * 128 + 128],
                    (pteo if par == 0 else pteo2)[:, QT:2 * QT],
                    start=(kc == 0), stop=(kc == NKC - 1)),
                    waits=wpo, inc=("tT", 1))
                self.pv_mark[par] = ("tT", v("tT"))
            for qt in range(NQT):
                q0 = qt * QT
                for p in range(4):
                    gp = hg * 4 + p
                    prev = None
                    for kc in range(NKC):
                        par = kc % 2
                        waits = self.wbank(0, [exp_done[0]]
                                           if exp_done[0] else [])
                        if qt == 0 and p == 0 and kc == 0:
                            waits += [v_sweep, s_sweep]
                        E("tensor", lambda T, p=p, kc=kc, q0=q0: T.matmul(
                            pb[0][:, 0:QT],
                            kt_[0:64, p * SEQ + kc * 128:p * SEQ + kc * 128 + 128],
                            qt_[0:64, p * SEQ + q0:p * SEQ + q0 + QT],
                            start=True, stop=True),
                            waits=waits, inc=("tT", 1))
                        E("tensor", lambda T, p=p, kc=kc, q0=q0: T.matmul(
                            pb[1][:, 0:QT],
                            kt_[64:128, p * SEQ + kc * 128:
                                p * SEQ + kc * 128 + 128],
                            qt_[64:128, p * SEQ + q0:p * SEQ + q0 + QT],
                            start=True, stop=True),
                            inc=("tT", 1))
                        tso = ("tT", v("tT"))
                        ow = [tso]
                        if self.pv_mark.get(par) is not None:
                            ow.append(self.pv_mark[par])
                        so = E("scalar", lambda Sc, par=par: Sc.activation(
                            (pteo if par == 0 else pteo2)[:, :],
                            self.pb01[:, 0:2 * QT], AF.Exp, scale=0.125,
                            bias=cneg[:, 0:1]), waits=ow, inc=("tS", 1))
                        exp_done[0] = so
                        exp_done[1] = so
                        self.bank_free[0] = so
                        self.bank_free[1] = so
                        if prev is not None:
                            emit_pv(prev, p)
                        if kc == 6 and self.pending_bcast is not None:
                            self.pending_bcast()
                            self.pending_bcast = None
                        prev = (so, so, kc)
                    emit_pv(prev, p)
                    t_pv = self.pv_mark[(NKC - 1) % 2]
                    self.bank_ready[2] = t_pv
                    self.bank_ready[3] = t_pv
                    stw = self.rbank(2, self.rbank(3, []))
                    stw += self.wbank(5, self.wbank(6, []))
                    E("vector", lambda V: V.tensor_copy(
                        pb[5][0:65, 0:QT], pb[2][0:65, 0:QT]), waits=stw)
                    vst = E("vector", lambda V: V.tensor_copy(
                        pb[6][:, 0:QT], pb[3][:, 0:QT]), inc=("tV", 1))
                    self.bank_free[2] = vst
                    self.bank_free[3] = vst
                    zcw = []
                    if self.zbsb_claim is not None:
                        zcw.append(self.zbsb_claim)
                        self.zbsb_claim = None
                    E("vector", lambda V: V.tensor_copy(
                        zbsb[64:65, 0:QT], pb[5][64:65, 0:QT]), waits=zcw)
                    E("vector", lambda V: V.tensor_copy(
                        zbsb[0:1, 0:QT], pb[6][0:1, 0:QT]))
                    E("vector", lambda V: V.reciprocal(
                        zrow[0:65, 0:QT], zbsb[0:65, 0:QT]))
                    E("vector", lambda V: V.tensor_copy(
                        zhl[64:65, 0:QT], zrow[64:65, 0:QT]))
                    E("vector", lambda V: V.tensor_tensor(
                        zhl[64:65, QT:2 * QT], zrow[64:65, 0:QT],
                        zhl[64:65, 0:QT], ALU.subtract))
                    E("vector", lambda V: V.tensor_copy(
                        zhl[0:1, 0:QT], zrow[0:1, 0:QT]))
                    vz = E("vector", lambda V: V.tensor_tensor(
                        zhl[0:1, QT:2 * QT], zrow[0:1, 0:QT],
                        zhl[0:1, 0:QT], ALU.subtract), inc=("tV", 1))

                    ob = (qt * 8 + gp) * QT

                    def tail_bcast(vz=vz, ob=ob):
                        E("tensor", lambda T: T.matmul(
                            pb[4][0:64, 0:QT], onesbf[64:65, 0:64],
                            zhl[64:65, 0:QT], start=True, stop=False),
                            waits=self.wbank(4, [vz]))
                        E("tensor", lambda T: T.matmul(
                            pb[4][0:64, 0:QT], onesbf[64:65, 0:64],
                            zhl[64:65, QT:2 * QT], start=False, stop=True))
                        E("tensor", lambda T: T.matmul(
                            pb[4][64:128, 0:QT], onesbf[0:1, 0:64],
                            zhl[0:1, 0:QT], start=True, stop=False))
                        tz = E("tensor", lambda T: T.matmul(
                            pb[4][64:128, 0:QT], onesbf[0:1, 0:64],
                            zhl[0:1, QT:2 * QT], start=False, stop=True),
                            inc=("tT", 1))
                        self.bank_ready[4] = tz
                        vzb = E("vector", lambda V: V.tensor_copy(
                            zbsb[:, 0:QT], pb[4][:, 0:QT]),
                            waits=self.rbank(4, []), inc=("tV", 1))
                        self.bank_free[4] = vzb
                        E("vector", lambda V: V.tensor_tensor(
                            osb[0:64, ob:ob + QT], pb[5][0:64, 0:QT],
                            zbsb[0:64, 0:QT], ALU.mult))
                        vo = E("vector", lambda V: V.tensor_tensor(
                            osb[64:128, ob:ob + QT], pb[6][64:128, 0:QT],
                            zbsb[64:128, 0:QT], ALU.mult), inc=("tV", 1))
                        self.bank_free[5] = vo
                        self.bank_free[6] = vo
                    self.pending_bcast = tail_bcast
            if self.pending_bcast is not None:
                self.pending_bcast()
                self.pending_bcast = None
        v_core = ("tV", v("tV"))

        srcw = bass.AP(self.wout, 0, [[DM, 128], [128 * DM, NDC], [1, DM]])
        vwo = dma("gpsimd", "dG", wo_sb[:, :],
                  srcw, waits=[("tT", v("tT")), v_core])
        for qt in range(NQT):
            q0 = qt * QT
            for dm in range(NDC):
                bank = 2 + (dm % 2)
                waits = [v_core, vwo] if (qt == 0 and dm == 0) else []
                waits = self.wbank(bank, waits)
                for dc in range(NDC):
                    E("tensor", lambda T, dm=dm, dc=dc, bank=bank, qt=qt:
                      T.matmul(
                          pb[bank][:, 0:QT],
                          wo_sb[:, dc * DM + dm * 128:dc * DM + dm * 128 + 128],
                          osb[:, (qt * 8 + dc) * QT:(qt * 8 + dc) * QT + QT],
                          start=(dc == 0), stop=(dc == NDC - 1)),
                        waits=waits if dc == 0 else [],
                        inc=("tT", 1) if dc == NDC - 1 else None)
                to = ("tT", v("tT"))
                self.bank_ready[bank] = to
                vo2 = E("vector", lambda V, dm=dm, bank=bank, q0=q0:
                        V.scalar_tensor_tensor(
                            self._xr(dm, q0, QT), pb[bank][:, 0:QT],
                            bo[:, dm:dm + 1], self._xr(dm, q0, QT),
                            ALU.add, ALU.add),
                        waits=self.rbank(bank, []),
                        inc=("tV", 1))
                self.bank_free[bank] = vo2


    def finalize(self):
        nc = self.nc
        sems = {}
        import contextlib
        stack = contextlib.ExitStack()
        for sname in ("dS", "dC", "dG", "tT", "tV", "tS", "tG"):
            sems[sname] = stack.enter_context(nc.semaphore(sname))
        block = stack.enter_context(nc.Block())

        def runner(eng_name):
            def run(eng):
                for waits, build, inc in self.ops[eng_name]:
                    for (sn, val) in waits:
                        eng.wait_ge(sems[sn], val)
                    instr = build(eng)
                    if inc is not None:
                        assert instr is not None, (eng_name, inc)
                        instr.then_inc(sems[inc[0]], inc[1])
            return run

        block.sync(runner("sync"))
        block.tensor(runner("tensor"))
        block.vector(runner("vector"))
        block.scalar(runner("scalar"))
        block.gpsimd(runner("gpsimd"))
        stack.close()
        return nc


def build_nc():
    b = Builder()
    return b.finalize()


_NC_CACHE = {}


def kernel(L0_attn_x, L0_attn_w_qkv, L0_attn_b_qkv, L0_attn_w_out, L0_attn_b_out,
           L0_ffn_w1, L0_ffn_conv_w, L0_ffn_conv_b, L0_ffn_w2):
    x = np.asarray(L0_attn_x, dtype=np.float32)
    wqkv = np.asarray(L0_attn_w_qkv, dtype=np.float32)
    wqkv_r = np.ascontiguousarray(wqkv.reshape(DM, 24, 128).transpose(1, 0, 2))
    bqkv = np.asarray(L0_attn_b_qkv, dtype=np.float32)
    wout = np.ascontiguousarray(np.asarray(L0_attn_w_out, dtype=np.float32))
    bout = np.asarray(L0_attn_b_out, dtype=np.float32)
    w1 = np.asarray(L0_ffn_w1, dtype=np.float32)
    convw = np.asarray(L0_ffn_conv_w, dtype=np.float32).reshape(DH, 3)
    convb = np.asarray(L0_ffn_conv_b, dtype=np.float32)
    w2 = np.asarray(L0_ffn_w2, dtype=np.float32)

    w1t = np.ascontiguousarray(w1.T)
    w2t = np.ascontiguousarray(w2.T)
    bqr = np.ascontiguousarray(bqkv.reshape(24, 128).T)
    bor = np.ascontiguousarray(bout.reshape(8, 128).T)
    cwr = np.ascontiguousarray(
        convw.reshape(32, 128, 3).transpose(1, 0, 2).reshape(128, 96))
    cbr = np.ascontiguousarray(convb.reshape(32, 128).T)

    if "nc" not in _NC_CACHE:
        _NC_CACHE["nc"] = build_nc()
    nc = _NC_CACHE["nc"]

    in_maps = []
    for c in range(8):
        xfm = np.ascontiguousarray(x[2 * c:2 * c + 2].transpose(0, 2, 1))
        in_maps.append({
            "xfm": xfm, "wqkv_r": wqkv_r, "wout": wout, "w1t": w1t, "w2t": w2t,
            "bqr": bqr, "bor": bor, "cwr": cwr, "cbr": cbr,
        })
    res = run_bass_kernel_spmd(nc, in_maps, core_ids=list(range(8)))
    out = np.empty((16, SEQ, DM), dtype=np.float32)
    for c in range(8):
        out[2 * c:2 * c + 2] = res.results[c]["ofm"].transpose(0, 2, 1)
    return out
